# version 08
# baseline (speedup 1.0000x reference)
import math
import numpy as np
import ml_dtypes
import concourse.bass as bass
import concourse.mybir as mybir
from concourse.bass_utils import run_bass_kernel_spmd

F32 = mybir.dt.float32
U8 = mybir.dt.uint8
BF16 = mybir.dt.bfloat16
AF = mybir.ActivationFunctionType
ALU = mybir.AluOpType
AXX = mybir.AxisListType.X

T = 4096
D = 1024
NCORE = 8
GL = 512
NG = T // GL
P = 256
TWOPI = 6.283185307179586
PI_CLAMP = 3.14159260
MAGIC = 12582912.0
GOLD_C = 1.0 / math.e
GOLD_L = 0.5 - math.log(4.0 / 3.0)
GOLD_U = 0.5
LN_EPS = 1e-5


class _RI:
    __slots__ = ("p",)
    def __init__(self, p):
        self.p = p

    def then_inc(self, sem_h, v):
        self.p.append((sem_h, v))
        return self


class Rec:
    def __init__(self):
        self.ops = []

    def op(self, name, *a, **k):
        pend = []
        self.ops.append((name, a, k, pend))
        return _RI(pend)

    def __getattr__(self, name):
        if name.startswith("_") or name == "ops":
            raise AttributeError(name)
        def call(*a, **k):
            return self.op(name, *a, **k)
        return call

    def replay(self, eng):
        for name, a, k, pend in self.ops:
            ins = getattr(eng, name)(*a, **k)
            for sem_h, v in pend:
                ins = ins.then_inc(sem_h, v)


class Sem:
    def __init__(self, nc, name):
        self.h = nc.alloc_semaphore(name)
        self.n = 0


def tick(instr, sem, k=1):
    instr.then_inc(sem.h, k)
    sem.n += k


def sv(c, instr):
    tick(instr, c.vS)
    c.ve.wait_ge(c.vS.h, c.vS.n)


class C:
    def __init__(self, nc):
        self.nc = nc
        self.sy = Rec()
        self.sc = Rec()
        self.ve = Rec()
        self.te = Rec()
        self.gp = Rec()
        self.dW = Sem(nc, "dW")
        self.dO = Sem(nc, "dO")
        self.dG = Sem(nc, "dG")
        self.tS = Sem(nc, "tS")
        self.vS = Sem(nc, "vS")
        self.sS = Sem(nc, "sS")
        self.gS = Sem(nc, "gS")
        self.bankg = [(None, 0)] * 8

    def bank_wait(self, i):
        s, n = self.bankg[i]
        if s is not None and n > 0:
            self.te.wait_ge(s.h, n)

    def fence(self, engines=None):
        for rec_ in engines or (self.sy, self.sc, self.ve, self.te):
            rec_.wait_ge(self.vS.h, self.vS.n)
            rec_.wait_ge(self.sS.h, self.sS.n)
            rec_.wait_ge(self.tS.h, self.tS.n)

    def run(self):
        with self.nc.Block() as block:
            block.sync(lambda e: self.sy.replay(e))
            block.scalar(lambda e: self.sc.replay(e))
            block.vector(lambda e: self.ve.replay(e))
            block.tensor(lambda e: self.te.replay(e))
            block.gpsimd(lambda e: self.gp.replay(e))


def emit_ln(c, src_aps, nbf_aps, onesb, banks, mbc, sbc, W,
            out_aps=None, gcols=None, bcols=None, tq=None):
    pb = c.pb
    bs, bq = banks
    v_src = c.vS.n
    cast_cnt = []
    for j in range(8):
        c.sc.wait_ge(c.vS.h, v_src)
        tick(c.sc.activation(nbf_aps[j], src_aps[j], AF.Copy), c.sS)
        cast_cnt.append(c.sS.n)
    c.bank_wait(bs)
    for j in range(8):
        c.te.wait_ge(c.sS.h, cast_cnt[j])
        mm = c.te.matmul(pb[bs][:, 0:W], onesb, nbf_aps[j],
                         start=(j == 0), stop=(j == 7))
    tick(mm, c.tS)
    t_sum = c.tS.n
    c.ve.wait_ge(c.tS.h, t_sum)
    tick(c.ve.tensor_scalar_mul(mbc, pb[bs][:, 0:W], 1.0 / 1024.0), c.vS)
    c.bankg[bs] = (c.vS, c.vS.n)
    for j in range(8):
        tick(c.ve.tensor_sub(src_aps[j], src_aps[j], mbc), c.vS)
    v_center = c.vS.n
    sq_cnt = []
    for j in range(8):
        c.sc.wait_ge(c.tS.h, t_sum)
        c.sc.wait_ge(c.vS.h, v_center)
        tick(c.sc.activation(nbf_aps[j], src_aps[j], AF.Square), c.sS)
        sq_cnt.append(c.sS.n)
    c.bank_wait(bq)
    for j in range(8):
        c.te.wait_ge(c.sS.h, sq_cnt[j])
        mm = c.te.matmul(pb[bq][:, 0:W], onesb, nbf_aps[j],
                         start=(j == 0), stop=(j == 7))
    tick(mm, c.tS)
    t_sq = c.tS.n
    c.sc.wait_ge(c.tS.h, t_sq)
    tick(c.sc.activation(sbc, pb[bq][:, 0:W], AF.Sqrt, scale=np.float32(1.0 / 1024.0),
                         bias=np.float32(LN_EPS)), c.sS)
    c.bankg[bq] = (c.sS, c.sS.n)
    c.ve.wait_ge(c.sS.h, c.sS.n)
    tick(c.ve.reciprocal(mbc, sbc), c.vS)
    if out_aps is not None:
        for j in range(8):
            if gcols is None:
                tick(c.ve.tensor_mul(out_aps[j], src_aps[j], mbc), c.vS)
            else:
                c.ve.wait_ge(c.sS.h, c.sS.n)
                tick(c.ve.tensor_mul(tq, src_aps[j], mbc), c.vS)
                c.sc.wait_ge(c.vS.h, c.vS.n)
                tick(c.sc.activation(out_aps[j], tq, AF.Identity,
                                     scale=gcols[j], bias=bcols[j]), c.sS)


def emit_cx_scan(c, xr, xi, tabs, cd, reverse):
    powAr, powAi, powAiN, p16r, p16i, p16iN = tabs
    v = c.ve

    def A1(k):
        return (powAr[:, cd, k - 1:k], powAi[:, cd, k - 1:k],
                powAiN[:, cd, k - 1:k])

    def A16(k):
        return (p16r[:, cd, k - 1:k], p16i[:, cd, k - 1:k],
                p16iN[:, cd, k - 1:k])

    def mac(out_r, out_i, in_r, in_i, A):
        ar, ai, ain = A
        tick(v.scalar_tensor_tensor(out_r, in_r, ar, out_r, ALU.mult, ALU.add), c.vS)
        tick(v.scalar_tensor_tensor(out_r, in_i, ain, out_r, ALU.mult, ALU.add), c.vS)
        tick(v.scalar_tensor_tensor(out_i, in_i, ar, out_i, ALU.mult, ALU.add), c.vS)
        tick(v.scalar_tensor_tensor(out_i, in_r, ai, out_i, ALU.mult, ALU.add), c.vS)

    v16r = xr.rearrange("p (k m) -> p k m", m=16)
    v16i = xi.rearrange("p (k m) -> p k m", m=16)
    vgr = xr.rearrange("p (g r m) -> p g r m", m=16, r=16)
    vgi = xi.rearrange("p (g r m) -> p g r m", m=16, r=16)
    vfr = xr.rearrange("p (g q) -> p g q", q=256)
    vfi = xi.rearrange("p (g q) -> p g q", q=256)

    if not reverse:
        for t in range(1, 16):
            mac(v16r[:, :, t], v16i[:, :, t],
                v16r[:, :, t - 1], v16i[:, :, t - 1], A1(1))
        for r in range(1, 16):
            mac(vgr[:, :, r, 15], vgi[:, :, r, 15],
                vgr[:, :, r - 1, 15], vgi[:, :, r - 1, 15], A16(1))
        for g in range(1, 16):
            mac(vfr[:, g, 255:256], vfi[:, g, 255:256],
                vfr[:, g - 1, 255:256], vfi[:, g - 1, 255:256], A16(16))
        for r in range(0, 15):
            mac(vgr[:, 1:16, r, 15], vgi[:, 1:16, r, 15],
                vfr[:, 0:15, 255], vfi[:, 0:15, 255], A16(r + 1))
        for t in range(0, 15):
            mac(v16r[:, 1:256, t], v16i[:, 1:256, t],
                v16r[:, 0:255, 15], v16i[:, 0:255, 15], A1(t + 1))
    else:
        for s in range(1, 16):
            mac(v16r[:, :, 15 - s], v16i[:, :, 15 - s],
                v16r[:, :, 16 - s], v16i[:, :, 16 - s], A1(1))
        for s in range(1, 16):
            mac(vgr[:, :, 15 - s, 0], vgi[:, :, 15 - s, 0],
                vgr[:, :, 16 - s, 0], vgi[:, :, 16 - s, 0], A16(1))
        for s in range(1, 16):
            mac(vfr[:, 15 - s, 0:1], vfi[:, 15 - s, 0:1],
                vfr[:, 16 - s, 0:1], vfi[:, 16 - s, 0:1], A16(16))
        for r in range(1, 16):
            mac(vgr[:, 0:15, r, 0], vgi[:, 0:15, r, 0],
                vfr[:, 1:16, 0], vfi[:, 1:16, 0], A16(16 - r))
        for t in range(1, 16):
            mac(v16r[:, 0:255, t], v16i[:, 0:255, t],
                v16r[:, 1:256, 0], v16i[:, 1:256, 0], A1(16 - t))


def emit_cx_mul(c, or_, oi, ar, ai, br, bi, t1, t2):
    g = c.gp
    tick(g.tensor_mul(t1, ar, br), c.gS)
    tick(g.tensor_mul(t2, ai, bi), c.gS)
    tick(g.tensor_sub(or_, t1, t2), c.gS)
    tick(g.tensor_mul(t1, ar, bi), c.gS)
    tick(g.tensor_mul(t2, ai, br), c.gS)
    tick(g.tensor_add(oi, t1, t2), c.gS)


def build_program(upto="all"):
    nc = bass.Bass("TRN2", target_bir_lowering=False,
                   detect_race_conditions=False)
    c = C(nc)
    from contextlib import ExitStack
    root = ExitStack()

    def din(name, shape, dtype=F32):
        return nc.dram_tensor(name, shape, dtype, kind="ExternalInput").ap()

    x_in = din("x", [T, D])
    wk = din("wk", [D, D]); bk = din("bk", [128, 8])
    wv = din("wv", [D, D]); bv = din("bv", [128, 8])
    lng = din("lng", [128, 8]); lnb = din("lnb", [128, 8])
    wo = din("wo", [D, D]); bo = din("bo", [128, 8])
    win = din("win", [D, 64]); binc = din("binc", [128, 1])
    w1d = din("w1d", [128, 16]); b1d = din("b1d", [128, 16])
    w2d = din("w2d", [128, 16]); b2d = din("b2d", [128, 1])
    woutd = din("woutd", [128, D]); boutc = din("boutc", [128, 8])
    lrc = {}; imc = {}; ldc = {}; brr = {}; bii = {}; crr = {}; cii = {}; dcol = {}
    for d_ in ("f", "b"):
        lrc[d_] = din("lr" + d_, [128, 2]); imc[d_] = din("im" + d_, [128, 2])
        ldc[d_] = din("ld" + d_, [128, 2])
        brr[d_] = din("br" + d_, [P, D]); bii[d_] = din("bi" + d_, [P, D])
        crr[d_] = din("cr" + d_, [P, D]); cii[d_] = din("ci" + d_, [P, D])
        dcol[d_] = din("dc" + d_, [128, 8])
    wg = din("wg", [8, D, D]); wu = din("wu", [8, D, D]); wd = din("wd", [8, D, D])
    wr = din("wr", [D, 8]); tempc = din("tempc", [128, 1])
    identf = din("identf", [128, 128], F32)
    identb = din("identb", [128, 128], BF16)
    onesb_in = din("onesb", [128, 128], BF16)
    bcast8 = din("bcast8", [8, D], BF16)

    out_t = nc.dram_tensor("out", [T, D], F32, kind="ExternalOutput").ap()

    xbf_h = nc.dram_tensor("xbf_h", [T, D], BF16, kind="Internal").ap()
    wbf_h = nc.dram_tensor("wbf_h", [8, 3, D, D], BF16, kind="Internal").ap()

    ES = root.enter_context
    resid = ES(nc.sbuf_tensor("resid", [128, 8, T], BF16))
    sb_identf = ES(nc.sbuf_tensor("sb_identf", [128, 128], F32))
    sb_identb = ES(nc.sbuf_tensor("sb_identb", [128, 128], BF16))
    sb_onesb = ES(nc.sbuf_tensor("sb_onesb", [128, 128], BF16))
    sp = {}
    for nm in ("bk", "bv", "lng", "lnb", "bo", "boutc", "dcf", "dcb"):
        sp[nm] = ES(nc.sbuf_tensor("sp_" + nm, [128, 8], F32))
    for nm in ("binc", "b2d", "tempc"):
        sp[nm] = ES(nc.sbuf_tensor("sp_" + nm, [128, 1], F32))
    for nm in ("w1d", "b1d", "w2d"):
        sp[nm] = ES(nc.sbuf_tensor("sp_" + nm, [128, 16], F32))
    for nm in ("lrf", "imf", "ldf", "lrb", "imb", "ldb"):
        sp[nm] = ES(nc.sbuf_tensor("sp_" + nm, [128, 2], F32))
    wrB = ES(nc.sbuf_tensor("wrB", [128, 8, 8], BF16))
    powAr = ES(nc.sbuf_tensor("powAr", [128, 4, 16], F32))
    powAi = ES(nc.sbuf_tensor("powAi", [128, 4, 16], F32))
    powAiN = ES(nc.sbuf_tensor("powAiN", [128, 4, 16], F32))
    p16r = ES(nc.sbuf_tensor("p16r", [128, 4, 16], F32))
    p16i = ES(nc.sbuf_tensor("p16i", [128, 4, 16], F32))
    p16iN = ES(nc.sbuf_tensor("p16iN", [128, 4, 16], F32))
    sBs = ES(nc.sbuf_tensor("sBs", [128, 4, 3], F32))
    ttc1 = ES(nc.sbuf_tensor("ttc1", [128, 2], F32))
    ttc2 = ES(nc.sbuf_tensor("ttc2", [128, 2], F32))
    ttc3 = ES(nc.sbuf_tensor("ttc3", [128, 2], F32))
    ttc4 = ES(nc.sbuf_tensor("ttc4", [128, 2], F32))
    cst = ES(nc.sbuf_tensor("cst", [128, 2, 4], F32))
    ttcm = ES(nc.sbuf_tensor("ttcm", [128, 2], U8))
    ttg1 = ES(nc.sbuf_tensor("ttg1", [128, 2], F32))
    ttg2 = ES(nc.sbuf_tensor("ttg2", [128, 2], F32))
    zz2 = ES(nc.sbuf_tensor("zz2", [128, 2, 16], F32))
    tabs = (powAr, powAi, powAiN, p16r, p16i, p16iN)

    c.pb = [ES(nc.psum_tensor(f"pbank{i}", [128, 512], F32)) for i in range(8)]
    pb = c.pb

    dGx = Sem(nc, "dGx"); dGk = Sem(nc, "dGk"); dGv = Sem(nc, "dGv")
    dGo = Sem(nc, "dGo"); dGc = Sem(nc, "dGc"); dGr = Sem(nc, "dGr")
    dGe = Sem(nc, "dGe"); dGi = Sem(nc, "dGi")
    dWs = Sem(nc, "dWs"); dWx = Sem(nc, "dWx"); dWkv = Sem(nc, "dWkv")
    dWwo = Sem(nc, "dWwo"); dWf = Sem(nc, "dWf"); dWct = Sem(nc, "dWct")
    dWb = Sem(nc, "dWb")
    dWsl = [Sem(nc, f"dWsl{i}") for i in range(4)]
    dOy = [Sem(nc, "dOy0"), Sem(nc, "dOy1")]
    tick(c.gp.dma_start(out=xbf_h, in_=x_in), dGx, 16)
    tick(c.gp.dma_start(out=wrB[:], in_=wr.rearrange("(kc p) e -> p kc e", p=128)),
         dGr, 16)
    _emit_expert_casts = lambda: [
        tick(c.gp.dma_start(out=wbf_h[e, m], in_=wsrc[e]), dGe, 16)
        for e in range(8) for m, wsrc in ((0, wg), (1, wu), (2, wd))]

    for nm, src in (("bk", bk), ("bv", bv), ("lng", lng), ("lnb", lnb),
                    ("bo", bo), ("boutc", boutc), ("dcf", dcol["f"]),
                    ("dcb", dcol["b"]), ("binc", binc), ("b2d", b2d),
                    ("tempc", tempc), ("w1d", w1d), ("b1d", b1d), ("w2d", w2d),
                    ("lrf", lrc["f"]), ("imf", imc["f"]), ("ldf", ldc["f"]),
                    ("lrb", lrc["b"]), ("imb", imc["b"]), ("ldb", ldc["b"])):
        tick(c.sy.dma_start(out=sp[nm][:], in_=src), dWs, 16)
    tick(c.sy.dma_start(out=sb_identf[:], in_=identf), dWs, 16)
    tick(c.sy.dma_start(out=sb_identb[:], in_=identb), dWs, 16)
    tick(c.sy.dma_start(out=sb_onesb[:], in_=onesb_in), dWs, 16)

    c.sy.wait_ge(dGx.h, 16)
    for ch in range(8):
        tick(c.sy.dma_start(out=resid[:, ch, :], in_=xbf_h[:, ch * 128:(ch + 1) * 128],
                            transpose=True), dWx, 16)
    c.dWx = dWx

    for rec_ in (c.sc, c.ve, c.te):
        rec_.wait_ge(dWs.h, dWs.n)

    l0 = ExitStack()
    E0 = l0.enter_context
    WkT = E0(nc.sbuf_tensor("WkT", [128, 8, D], BF16))
    WvT = E0(nc.sbuf_tensor("WvT", [128, 8, D], BF16))
    WoT = E0(nc.sbuf_tensor("WoT", [128, 8, D], BF16))
    posrow = E0(nc.sbuf_tensor("posrow", [128, T], BF16))
    kys2 = E0(nc.sbuf_tensor("kys2", [128, 2, 8, GL], F32))
    vrt = E0(nc.sbuf_tensor("vrt", [128, 8, GL], F32))
    bnd = E0(nc.sbuf_tensor("bnd", [128, 8, GL], F32))
    tq1 = E0(nc.sbuf_tensor("tq1", [128, GL], F32))
    tq2 = E0(nc.sbuf_tensor("tq2", [128, GL], F32))
    mbc = E0(nc.sbuf_tensor("mbc", [128, GL], F32))
    sbc = E0(nc.sbuf_tensor("sbc", [128, GL], F32))
    ones512 = E0(nc.sbuf_tensor("ones512", [128, GL], F32))
    carry = E0(nc.sbuf_tensor("carry", [128, 8], F32))

    tick(c.gp.dma_start(out=WkT[:], in_=wk.rearrange("(kc p) m -> p kc m", p=128)),
         dWkv, 16)
    tick(c.gp.dma_start(out=WvT[:], in_=wv.rearrange("(kc p) m -> p kc m", p=128)),
         dWkv, 16)
    tick(c.gp.dma_start(out=WoT[:], in_=wo.rearrange("(kc p) m -> p kc m", p=128)),
         dWwo, 16)

    c.ve.wait_ge(dWs.h, dWs.n)
    tick(c.ve.memset(zz2[:], 0.0), c.vS)

    def rangered(dst, src_ap, shift):
        v = c.ve
        sv(c, v.tensor_scalar(ttc2[:], src_ap, 1.0 / TWOPI, shift / TWOPI,
                              ALU.mult, ALU.add))
        sv(c, v.tensor_scalar_add(ttc2[:], ttc2[:], MAGIC))
        sv(c, v.tensor_scalar_add(ttc2[:], ttc2[:], -MAGIC))
        sv(c, v.tensor_scalar_mul(ttc2[:], ttc2[:], -TWOPI))
        sv(c, v.tensor_add(ttc2[:], ttc2[:], src_ap))
        sv(c, v.tensor_scalar_add(ttc2[:], ttc2[:], shift))
        sv(c, v.tensor_scalar_min(ttc2[:], ttc2[:], PI_CLAMP))
        sv(c, v.tensor_scalar_max(dst, ttc2[:], -PI_CLAMP))

    for dnum, d_ in enumerate(("f", "b")):
        lr_, im_, ldl = sp["lr" + d_][:], sp["im" + d_][:], sp["ld" + d_][:]
        cdsl = slice(dnum * 2, dnum * 2 + 2)
        v, s = c.ve, c.sc
        c.sc.wait_ge(c.vS.h, c.vS.n)
        tick(s.activation(ttc1[:], ldl, AF.Exp), c.sS)
        tick(s.activation(ttc2[:], lr_, AF.Exp), c.sS)
        c.ve.wait_ge(c.sS.h, c.sS.n)
        sv(c, v.tensor_mul(ttc3[:], ttc2[:], ttc1[:]))
        sv(c, v.tensor_scalar_mul(ttc3[:], ttc3[:], -1.0))
        sv(c, v.tensor_mul(ttc4[:], im_, ttc1[:]))
        c.sc.wait_ge(c.vS.h, c.vS.n)
        tick(s.activation(ttc1[:], ttc3[:], AF.Exp), c.sS)
        c.ve.wait_ge(c.sS.h, c.sS.n)
        rangered(ttc3[:], ttc4[:], 0.0)
        c.sc.wait_ge(c.vS.h, c.vS.n)
        tick(s.activation(ttc3[:], ttc3[:], AF.Sin), c.sS)
        c.ve.wait_ge(c.sS.h, c.sS.n)
        rangered(ttc2[:], ttc4[:], math.pi / 2)
        c.sc.wait_ge(c.vS.h, c.vS.n)
        tick(s.activation(ttc4[:], ttc2[:], AF.Sin), c.sS)
        c.ve.wait_ge(c.sS.h, c.sS.n)
        sv(c, v.tensor_mul(powAr[:, cdsl, 0], ttc1[:], ttc4[:]))
        sv(c, v.tensor_mul(powAi[:, cdsl, 0], ttc1[:], ttc3[:]))
        sv(c, v.tensor_mul(ttc1[:], im_, im_))
        c.sc.wait_ge(c.vS.h, c.vS.n)
        tick(s.activation(ttc2[:], lr_, AF.Exp), c.sS)
        c.ve.wait_ge(c.sS.h, c.sS.n)
        sv(c, v.tensor_mul(ttc3[:], ttc2[:], ttc2[:]))
        sv(c, v.tensor_add(ttc1[:], ttc1[:], ttc3[:]))
        sv(c, v.tensor_scalar(ttcm[:], ttc1[:], 1e-12, None, ALU.is_lt))
        sv(c, v.tensor_scalar_mul(ttc2[:], ttc2[:], -1.0))
        sv(c, v.memset(cst[:, :, 0], 1e-6))
        sv(c, v.memset(cst[:, :, 1], 0.0))
        sv(c, v.select(ttc2[:], ttcm[:], cst[:, :, 0], ttc2[:]))
        sv(c, v.select(ttc4[:], ttcm[:], cst[:, :, 1], im_))
        sv(c, v.tensor_mul(ttc1[:], ttc2[:], ttc2[:]))
        sv(c, v.tensor_mul(ttc3[:], ttc4[:], ttc4[:]))
        sv(c, v.tensor_add(ttc1[:], ttc1[:], ttc3[:]))
        sv(c, v.reciprocal(ttc1[:], ttc1[:]))
        sv(c, v.tensor_scalar_add(ttc3[:], powAr[:, cdsl, 0], -1.0))
        sv(c, v.tensor_mul(cst[:, :, 2], ttc3[:], ttc2[:]))
        sv(c, v.tensor_mul(cst[:, :, 3], powAi[:, cdsl, 0], ttc4[:]))
        sv(c, v.tensor_add(cst[:, :, 2], cst[:, :, 2], cst[:, :, 3]))
        sv(c, v.tensor_mul(sBs[:, cdsl, 0], cst[:, :, 2], ttc1[:]))
        sv(c, v.tensor_mul(cst[:, :, 2], powAi[:, cdsl, 0], ttc2[:]))
        sv(c, v.tensor_mul(cst[:, :, 3], ttc3[:], ttc4[:]))
        sv(c, v.tensor_sub(cst[:, :, 2], cst[:, :, 2], cst[:, :, 3]))
        sv(c, v.tensor_mul(sBs[:, cdsl, 1], cst[:, :, 2], ttc1[:]))
        sv(c, v.tensor_scalar_mul(sBs[:, cdsl, 2], sBs[:, cdsl, 1], -1.0))
        c.gp.wait_ge(c.vS.h, c.vS.n)
        for k in range(1, 16):
            emit_cx_mul(c, powAr[:, cdsl, k], powAi[:, cdsl, k],
                        powAr[:, cdsl, k - 1], powAi[:, cdsl, k - 1],
                        powAr[:, cdsl, 0], powAi[:, cdsl, 0], ttg1[:], ttg2[:])
        tick(c.gp.tensor_copy(p16r[:, cdsl, 0], powAr[:, cdsl, 15]), c.gS)
        tick(c.gp.tensor_copy(p16i[:, cdsl, 0], powAi[:, cdsl, 15]), c.gS)
        for k in range(1, 16):
            emit_cx_mul(c, p16r[:, cdsl, k], p16i[:, cdsl, k],
                        p16r[:, cdsl, k - 1], p16i[:, cdsl, k - 1],
                        p16r[:, cdsl, 0], p16i[:, cdsl, 0], ttg1[:], ttg2[:])
        tick(c.gp.tensor_sub(powAiN[:, cdsl, :], zz2[:], powAi[:, cdsl, :]), c.gS)
        tick(c.gp.tensor_sub(p16iN[:, cdsl, :], zz2[:], p16i[:, cdsl, :]), c.gS)


    onesb_ap = sb_onesb[:]

    tick(c.ve.memset(ones512[:], 1.0), c.vS)
    tick(c.ve.memset(carry[:], 0.0), c.vS)
    bndflat = bnd[:].rearrange("p a b -> p (a b)")
    tick(c.gp.iota(bndflat, pattern=[[1, T]], base=1, channel_multiplier=0,
                   allow_small_or_imprecise_dtypes=True), dGi, 1)
    _emit_expert_casts()
    c.sc.wait_ge(dGi.h, 1)
    tick(c.sc.activation(bndflat, bndflat, AF.Sqrt), c.sS)
    c.ve.wait_ge(c.sS.h, c.sS.n)
    tick(c.ve.reciprocal(bndflat, bndflat), c.vS)
    tick(c.ve.tensor_copy(posrow[:], bndflat), c.vS)

    QM = [
        (0, [(0, 0, +1), (1, 1, -1), (2, 2, -1), (3, 3, -1)]),
        (1, [(0, 1, +1), (1, 0, +1), (2, 3, +1), (3, 2, -1)]),
        (2, [(0, 2, +1), (1, 3, -1), (2, 0, +1), (3, 1, +1)]),
        (3, [(0, 3, +1), (1, 2, +1), (2, 1, -1), (3, 0, +1)]),
    ]
    QMC = [
        (0, [(0, 0, +1), (1, 1, +1), (2, 2, +1), (3, 3, +1)]),
        (1, [(1, 0, +1), (3, 2, +1), (0, 1, -1), (2, 3, -1)]),
        (2, [(2, 0, +1), (1, 3, +1), (0, 2, -1), (3, 1, -1)]),
        (3, [(3, 0, +1), (2, 1, +1), (0, 3, -1), (1, 2, -1)]),
    ]

    def emit_quat(table, a_t, b_t, o_t, qh):
        def pl(comp):
            return comp * 2 + qh
        v = c.ve
        c.gp.wait_ge(c.vS.h, c.vS.n)
        for oc, terms in table:
            (a0, b0, s0), (a1, b1, s1), (a2, b2, s2), (a3, b3, s3) = terms
            assert s0 > 0
            e_, sem, u1, u2 = v, c.vS, tq1[:], tq2[:]
            tick(e_.tensor_mul(u1, a_t[:, pl(a0), :], b_t[:, pl(b0), :]), sem)
            tick(e_.tensor_mul(u2, a_t[:, pl(a1), :], b_t[:, pl(b1), :]), sem)
            tick(e_.tensor_tensor(u1, u1, u2,
                                  ALU.add if s1 > 0 else ALU.subtract), sem)
            tick(e_.tensor_mul(u2, a_t[:, pl(a2), :], b_t[:, pl(b2), :]), sem)
            tick(e_.tensor_tensor(u1, u1, u2,
                                  ALU.add if s2 > 0 else ALU.subtract), sem)
            tick(e_.tensor_mul(u2, a_t[:, pl(a3), :], b_t[:, pl(b3), :]), sem)
            tick(e_.tensor_tensor(o_t[:, pl(oc), :], u1, u2,
                                  ALU.add if s3 > 0 else ALU.subtract), sem)

    bndbf = bnd[:].rearrange("p a b -> p (a b)").bitcast(BF16)

    def nbv(j):
        return bndbf[:, j * 2 * GL: j * 2 * GL + GL]

    kguard = [0, 0]
    kguard_g = [0, 0]
    vrt_guard = 0
    vrt_guard_g = [0]
    kv_ready = {}
    vv_ready = {}

    def emit_A_keys(g):
        p = g % 2
        gsl = slice(g * GL, (g + 1) * GL)
        for pl_ in range(8):
            bank = pl_ % 2
            c.bank_wait(bank)
            c.te.wait_ge(dWkv.h, 32)
            c.te.wait_ge(dWx.h, 128)
            for kc in range(8):
                mm = c.te.matmul(pb[bank][:, 0:GL],
                                 WkT[:, kc, pl_ * 128:(pl_ + 1) * 128],
                                 resid[:, kc, gsl], start=(kc == 0), stop=(kc == 7))
            tick(mm, c.tS)
            c.sc.wait_ge(c.tS.h, c.tS.n)
            c.sc.wait_ge(c.vS.h, kguard[p])
            c.sc.wait_ge(c.gS.h, kguard_g[p])
            tick(c.sc.activation(kys2[:, p, pl_, :], pb[bank][:, 0:GL], AF.Identity,
                                 bias=sp["bk"][:, pl_:pl_ + 1]), c.sS)
            c.bankg[bank] = (c.sS, c.sS.n)
        kv_ready[g] = c.sS.n

    def emit_A_values(g):
        gsl = slice(g * GL, (g + 1) * GL)
        for pl_ in range(8):
            bank = 2 + (pl_ % 2)
            c.bank_wait(bank)
            for kc in range(8):
                mm = c.te.matmul(pb[bank][:, 0:GL],
                                 WvT[:, kc, pl_ * 128:(pl_ + 1) * 128],
                                 resid[:, kc, gsl], start=(kc == 0), stop=(kc == 7))
            tick(mm, c.tS)
            c.sc.wait_ge(c.tS.h, c.tS.n)
            c.sc.wait_ge(c.vS.h, vrt_guard)
            c.sc.wait_ge(c.gS.h, vrt_guard_g[0])
            tick(c.sc.activation(vrt[:, pl_, :], pb[bank][:, 0:GL], AF.Identity,
                                 bias=sp["bv"][:, pl_:pl_ + 1]), c.sS)
            c.bankg[bank] = (c.sS, c.sS.n)
        vv_ready[g] = c.sS.n

    for g in range(NG):
        p = g % 2
        gsl = slice(g * GL, (g + 1) * GL)
        kys = kys2[:, p]
        if g == 0:
            emit_A_keys(0)
            emit_A_values(0)
        if g + 1 < NG:
            emit_A_keys(g + 1)

        c.ve.wait_ge(c.sS.h, max(kv_ready[g], vv_ready[g]))
        for qh in range(2):
            pls = [cc * 2 + qh for cc in range(4)]
            v = c.ve
            tick(v.tensor_mul(mbc[:], kys[:, pls[0], :], kys[:, pls[0], :]), c.vS)
            for cc in range(1, 4):
                tick(v.tensor_mul(tq1[:], kys[:, pls[cc], :], kys[:, pls[cc], :]), c.vS)
                tick(v.tensor_add(mbc[:], mbc[:], tq1[:]), c.vS)
            c.sc.wait_ge(c.vS.h, c.vS.n)
            tick(c.sc.activation(sbc[:], mbc[:], AF.Sqrt), c.sS)
            c.ve.wait_ge(c.sS.h, c.sS.n)
            tick(v.tensor_scalar_max(sbc[:], sbc[:], 1e-12), c.vS)
            tick(v.reciprocal(tq1[:], sbc[:]), c.vS)
            for cc in range(4):
                tick(v.tensor_mul(kys[:, pls[cc], :], kys[:, pls[cc], :], tq1[:]), c.vS)
            emit_quat(QM, vrt, kys, bnd, qh)

        c.ve.wait_ge(c.vS.h, c.vS.n)
        c.ve.wait_ge(c.gS.h, c.gS.n)
        for pl_ in range(8):
            tick(c.ve.tensor_tensor_scan(bnd[:, pl_, :], ones512[:], bnd[:, pl_, :],
                                         carry[:, pl_:pl_ + 1], ALU.mult, ALU.add),
                 c.vS)
        c.ve.wait_ge(c.vS.h, c.vS.n)
        for pl_ in range(8):
            tick(c.ve.tensor_copy(carry[:, pl_:pl_ + 1], bnd[:, pl_, GL - 1:GL]), c.vS)
        for pl_ in range(8):
            tick(c.ve.tensor_mul(bnd[:, pl_, :], bnd[:, pl_, :], posrow[:, gsl]), c.vS)
        for qh in range(2):
            emit_quat(QMC, bnd, kys, vrt, qh)
        kguard[p] = c.vS.n
        kguard_g[p] = c.gS.n

        c.sc.wait_ge(c.gS.h, c.gS.n)
        c.ve.wait_ge(c.gS.h, c.gS.n)
        src_ = [vrt[:, j, :] for j in range(8)]
        nb = [nbv(j) for j in range(8)]
        gcs = [sp["lng"][:, j:j + 1] for j in range(8)]
        bcs = [sp["lnb"][:, j:j + 1] for j in range(8)]
        emit_ln(c, src_, nb, onesb_ap, (4, 5), mbc[:], sbc[:], GL,
                out_aps=nb, gcols=gcs, bcols=bcs, tq=tq1[:])
        vrt_guard = c.vS.n
        vrt_guard_g[0] = c.gS.n
        sS_normed = c.sS.n

        if g + 1 < NG:
            emit_A_values(g + 1)

        for mc in range(8):
            bank = 6 + (mc % 2)
            c.bank_wait(bank)
            c.te.wait_ge(c.sS.h, sS_normed)
            c.te.wait_ge(dWwo.h, 16)
            for kc in range(8):
                mm = c.te.matmul(pb[bank][:, 0:GL],
                                 WoT[:, kc, mc * 128:(mc + 1) * 128],
                                 nbv(kc), start=(kc == 0), stop=(kc == 7))
            tick(mm, c.tS)
            c.sc.wait_ge(c.tS.h, c.tS.n)
            c.sc.wait_ge(c.vS.h, kguard[p])
            tick(c.sc.activation(kys[:, mc, :], pb[bank][:, 0:GL], AF.Identity,
                                 bias=sp["bo"][:, mc:mc + 1]), c.sS)
            c.bankg[bank] = (c.sS, c.sS.n)
            c.ve.wait_ge(c.sS.h, c.sS.n)
            tick(c.ve.scalar_tensor_tensor(kys[:, mc, :], resid[:, mc, gsl], 2.0,
                                           kys[:, mc, :], ALU.mult, ALU.add), c.vS)
        src_ = [kys[:, j, :] for j in range(8)]
        outs = [resid[:, j, gsl] for j in range(8)]
        emit_ln(c, src_, nb, onesb_ap, (4, 5), mbc[:], sbc[:], GL, out_aps=outs)
        kguard[p] = c.vS.n

    if upto == "l0a":
        _emit_debug_out(c, resid, out_t, l0, nc)
        return _finish(c, nc)

    l0.close()
    l0f = ExitStack()
    E1 = l0f.enter_context
    winB = E1(nc.sbuf_tensor("winB", [128, 8, 64], BF16))
    woutB = E1(nc.sbuf_tensor("woutB", [128, D], BF16))
    wstage = E1(nc.sbuf_tensor("wstage", [128, D], F32))
    win32 = E1(nc.sbuf_tensor("win32", [128, 8, 64], F32))
    s2 = E1(nc.sbuf_tensor("s2", [128, 2048], F32))
    g2 = E1(nc.sbuf_tensor("g2", [128, 2048], F32))
    g2b = E1(nc.sbuf_tensor("g2b", [128, 2048], BF16))
    h2 = E1(nc.sbuf_tensor("h2", [128, 2, 2048], F32))
    sf = E1(nc.sbuf_tensor("sf", [128, 8, 512], F32))
    nbf2 = E1(nc.sbuf_tensor("nbf2", [128, 8, 512], BF16))
    mbc2 = E1(nc.sbuf_tensor("mbc2", [128, 512], F32))
    sbc2 = E1(nc.sbuf_tensor("sbc2", [128, 512], F32))
    c.fence()

    tick(c.sy.dma_start(out=win32[:], in_=win.rearrange("(kc p) m -> p kc m", p=128)),
         dWf, 16)
    tick(c.sy.dma_start(out=wstage[:], in_=woutd), dWf, 16)
    c.ve.wait_ge(dWf.h, 32)
    tick(c.ve.tensor_copy(winB[:], win32[:]), c.vS)
    tick(c.ve.tensor_copy(woutB[:], wstage[:]), c.vS)
    vS_l0fw = c.vS.n

    for tt in range(8):
        bank = tt % 2
        half = tt // 4
        prow = slice(64 * half, 64 * half + 64)
        col = slice((tt % 4) * 512, (tt % 4) * 512 + 512)
        c.bank_wait(bank)
        c.te.wait_ge(c.vS.h, vS_l0fw)
        for kc in range(8):
            mm = c.te.matmul(pb[bank][prow, 0:512], winB[:, kc, 0:64],
                             resid[:, kc, tt * 512:(tt + 1) * 512],
                             start=(kc == 0), stop=(kc == 7))
        tick(mm, c.tS)
        c.sc.wait_ge(c.tS.h, c.tS.n)
        tick(c.sc.activation(s2[prow, col], pb[bank][prow, 0:512], AF.Identity,
                             bias=sp["binc"][prow, 0:1]), c.sS)
        c.bankg[bank] = (c.sS, c.sS.n)
    sS_s2 = c.sS.n

    c.sc.wait_ge(c.sS.h, sS_s2)
    tick(c.sc.activation(g2[:], s2[:], AF.Identity, scale=np.float32(0.0), bias=sp["b2d"][:, 0:1]),
         c.sS)
    for i in range(16):
        c.sc.wait_ge(c.vS.h, c.vS.n)
        tick(c.sc.activation(h2[:, i % 2, :], s2[:], AF.Relu,
                             scale=sp["w1d"][:, i:i + 1], bias=sp["b1d"][:, i:i + 1]),
             c.sS)
        c.ve.wait_ge(c.sS.h, c.sS.n)
        tick(c.ve.scalar_tensor_tensor(g2[:], h2[:, i % 2, :], sp["w2d"][:, i:i + 1],
                                       g2[:], ALU.mult, ALU.add), c.vS)
    tick(c.ve.tensor_copy(g2b[:], g2[:]), c.vS)
    vS_g2b = c.vS.n

    for tt in range(8):
        half = tt // 4
        prow = slice(64 * half, 64 * half + 64)
        col = slice((tt % 4) * 512, (tt % 4) * 512 + 512)
        for mc in range(8):
            bank = 2 + (mc % 2)
            c.bank_wait(bank)
            c.te.wait_ge(c.vS.h, vS_g2b)
            mm = c.te.matmul(pb[bank][:, 0:512], woutB[prow, mc * 128:(mc + 1) * 128],
                             g2b[prow, col], start=True, stop=True)
            tick(mm, c.tS)
            c.sc.wait_ge(c.tS.h, c.tS.n)
            c.sc.wait_ge(c.vS.h, c.vS.n)
            tick(c.sc.activation(sf[:, mc, :], pb[bank][:, 0:512], AF.Identity,
                                 bias=sp["boutc"][:, mc:mc + 1]), c.sS)
            c.bankg[bank] = (c.sS, c.sS.n)
            c.ve.wait_ge(c.sS.h, c.sS.n)
            tick(c.ve.scalar_tensor_tensor(sf[:, mc, :],
                                           resid[:, mc, tt * 512:(tt + 1) * 512],
                                           1.0, sf[:, mc, :], ALU.mult, ALU.add),
                 c.vS)
        src = [sf[:, j, :] for j in range(8)]
        nb = [nbf2[:, j, :] for j in range(8)]
        outs = [resid[:, j, tt * 512:(tt + 1) * 512] for j in range(8)]
        emit_ln(c, src, nb, onesb_ap, (4, 5), mbc2[:], sbc2[:], 512, out_aps=outs)

    if upto == "l0f":
        _emit_debug_out(c, resid, out_t, l0f, nc)
        return _finish(c, nc)

    l0f.close()
    l1a = ExitStack()
    E2 = l1a.enter_context
    xsb = {}
    for d_ in ("f", "b"):
        xsb["r" + d_] = E2(nc.sbuf_tensor("xsr" + d_, [128, 2, T], BF16))
        xsb["i" + d_] = E2(nc.sbuf_tensor("xsi" + d_, [128, 2, T], BF16))
    bur = E2(nc.sbuf_tensor("bur", [128, 2, T], F32))
    BT = {"r": E2(nc.sbuf_tensor("BTr", [128, 8, P], BF16)),
          "i": E2(nc.sbuf_tensor("BTi", [128, 8, P], BF16))}
    l1a_mid = ExitStack()
    bui = l1a_mid.enter_context(nc.sbuf_tensor("bui", [128, 2, T], F32))
    braw = bur[:, :, 0:D]
    biraw = bui[:, :, 0:D]
    btmp1 = bur[:, :, D:2 * D]
    btmp2 = bui[:, :, D:2 * D]
    c.fence()

    l1dump = {}
    if upto == "l1adbg":
        dOd3 = Sem(nc, "dOd3")
        l1dump["tabs"] = nc.dram_tensor("d_tabs", [128, 4, 64], F32,
                                        kind="ExternalOutput").ap()
        l1dump["sbs"] = nc.dram_tensor("d_sbs", [128, 4, 3], F32,
                                       kind="ExternalOutput").ap()
        l1dump["bt"] = nc.dram_tensor("d_bt", [128, 8, 512], F32,
                                      kind="ExternalOutput").ap()
        l1dump["bu"] = nc.dram_tensor("d_bu", [128, 4, T], F32,
                                      kind="ExternalOutput").ap()
        l1dump["xs"] = nc.dram_tensor("d_xs", [128, 4, T], F32,
                                      kind="ExternalOutput").ap()

        def l1d(dst_ap, src_ap):
            c.gp.wait_ge(c.vS.h, c.vS.n)
            c.gp.wait_ge(c.sS.h, c.sS.n)
            tick(c.gp.dma_start(out=dst_ap, in_=src_ap), dOd3, 16)
            c.ve.wait_ge(dOd3.h, dOd3.n)
            c.sc.wait_ge(dOd3.h, dOd3.n)
        l1d(l1dump["tabs"][:, :, 0:16], powAr[:])
        l1d(l1dump["tabs"][:, :, 16:32], powAi[:])
        l1d(l1dump["tabs"][:, :, 32:48], p16r[:])
        l1d(l1dump["tabs"][:, :, 48:64], p16i[:])
        l1d(l1dump["sbs"], sBs[:])

    for dnum, d_ in enumerate(("f", "b")):
        c.sy.wait_ge(c.vS.h, c.vS.n)
        tick(c.sy.dma_start(out=braw,
                            in_=brr[d_].rearrange("(a p) h -> p a h", p=128)),
             dWb, 16)
        tick(c.sy.dma_start(out=biraw,
                            in_=bii[d_].rearrange("(a p) h -> p a h", p=128)),
             dWb, 16)
        c.ve.wait_ge(dWb.h, dWb.n)
        v = c.ve
        for ch in range(2):
            cd = dnum * 2 + ch
            sr = sBs[:, cd, 0:1]
            si = sBs[:, cd, 1:2]
            siN = sBs[:, cd, 2:3]
            sv(c, v.tensor_scalar_mul(btmp1[:, ch, :], braw[:, ch, :], sr))
            sv(c, v.scalar_tensor_tensor(btmp1[:, ch, :], biraw[:, ch, :], siN,
                                         btmp1[:, ch, :], ALU.mult, ALU.add))
            sv(c, v.tensor_scalar_mul(btmp2[:, ch, :], biraw[:, ch, :], sr))
            sv(c, v.scalar_tensor_tensor(btmp2[:, ch, :], braw[:, ch, :], si,
                                         btmp2[:, ch, :], ALU.mult, ALU.add))
        vS_bb = c.vS.n
        for comp, src_t in (("r", btmp1), ("i", btmp2)):
            for ch in range(2):
                for hb in range(8):
                    bank = hb % 2
                    c.bank_wait(bank)
                    c.te.wait_ge(c.vS.h, vS_bb)
                    tick(c.te.transpose(pb[bank][:, 0:128],
                                        src_t[:, ch, hb * 128:(hb + 1) * 128],
                                        sb_identf[:]), c.tS)
                    c.ve.wait_ge(c.tS.h, c.tS.n)
                    tick(c.ve.tensor_copy(BT[comp][:, hb, ch * 128:(ch + 1) * 128],
                                          pb[bank][:, 0:128]), c.vS)
                    c.bankg[bank] = (c.vS, c.vS.n)
        vS_bt = c.vS.n

        if upto == "l1adbg" and dnum == 0:
            c.gp.wait_ge(c.vS.h, c.vS.n)
            tick(c.gp.dma_start(out=l1dump["bt"][:, :, 0:256].bitcast(BF16)[:, :, 0:256]
                                if False else l1dump["bt"][:, :, 0:256],
                                in_=BT["r"][:]), dOd3, 16) if False else None
        vS_before_bu = c.vS.n
        for comp, dst in (("r", bur), ("i", bui)):
            for pc in range(2):
                for tt in range(8):
                    bank = 2 + (tt % 2)
                    c.bank_wait(bank)
                    c.te.wait_ge(c.vS.h, vS_bt)
                    for kc in range(8):
                        mm = c.te.matmul(pb[bank][:, 0:512],
                                         BT[comp][:, kc, pc * 128:(pc + 1) * 128],
                                         resid[:, kc, tt * 512:(tt + 1) * 512],
                                         start=(kc == 0), stop=(kc == 7))
                    tick(mm, c.tS)
                    c.sc.wait_ge(c.tS.h, c.tS.n)
                    c.sc.wait_ge(c.vS.h, vS_before_bu)
                    tick(c.sc.activation(dst[:, pc, tt * 512:(tt + 1) * 512],
                                         pb[bank][:, 0:512], AF.Copy), c.sS)
                    c.bankg[bank] = (c.sS, c.sS.n)
        c.ve.wait_ge(c.sS.h, c.sS.n)
        if upto == "l1adbg" and dnum == 0:
            l1d(l1dump["bu"][:, 0:2, :], bur[:])
            l1d(l1dump["bu"][:, 2:4, :], bui[:])
        c.ve.wait_ge(c.gS.h, c.gS.n)
        for ch in range(2):
            emit_cx_scan(c, bur[:, ch, :], bui[:, ch, :], tabs, dnum * 2 + ch,
                         reverse=(d_ == "b"))
        if upto == "l1adbg" and dnum == 0:
            l1d(l1dump["xs"][:, 0:2, :], bur[:])
            l1d(l1dump["xs"][:, 2:4, :], bui[:])
        for ch in range(2):
            tick(c.ve.tensor_copy(xsb["r" + d_][:, ch, :], bur[:, ch, :]), c.vS)
            tick(c.ve.tensor_copy(xsb["i" + d_][:, ch, :], bui[:, ch, :]), c.vS)

    l1a_mid.close()
    E2b = l1a.enter_context
    CT = {}
    for d_ in ("f", "b"):
        CT["r" + d_] = E2b(nc.sbuf_tensor("CTr" + d_, [128, 2, D], BF16))
        CT["i" + d_] = E2b(nc.sbuf_tensor("CTi" + d_, [128, 2, D], BF16))
    nb3b = E2b(nc.sbuf_tensor("nb3b", [128, 8, 512], BF16))
    mbc3 = E2b(nc.sbuf_tensor("mbc3", [128, 512], F32))
    sbc3 = E2b(nc.sbuf_tensor("sbc3", [128, 512], F32))
    vS_pre_ct = c.vS.n
    c.gp.wait_ge(c.vS.h, vS_pre_ct)
    for d_ in ("f", "b"):
        for comp, csrc in (("r", crr[d_]), ("i", cii[d_])):
            tick(c.gp.dma_start(out=CT[comp + d_][:],
                                in_=csrc.rearrange("(pc p) m -> p pc m", p=128)),
                 dWct, 16)
    c.ve.wait_ge(dWct.h, 64)
    for d_ in ("f", "b"):
        cta = CT["i" + d_][:].rearrange("p a b -> p (a b)")
        tick(c.ve.tensor_scalar_mul(cta, cta, -1.0), c.vS)
    diagD = {"f": E2b(nc.sbuf_tensor("diagDf", [128, 8, 128], BF16)),
             "b": E2b(nc.sbuf_tensor("diagDb", [128, 8, 128], BF16))}
    for d_ in ("f", "b"):
        dk = "dcf" if d_ == "f" else "dcb"
        for mc in range(8):
            tick(c.ve.tensor_scalar_mul(diagD[d_][:, mc, :], sb_identb[:],
                                        sp[dk][:, mc:mc + 1]), c.vS)
    sfv = bur[:].rearrange("p a t -> p (a t)").rearrange("p (m w) -> p m w", w=512)
    xs_done = c.vS.n
    sf_guard = xs_done
    for tt in range(8):
        tsl = slice(tt * 512, (tt + 1) * 512)
        for mc in range(8):
            for dnum, d_ in enumerate(("f", "b")):
                bank = 4 + ((mc + dnum) % 2)
                c.bank_wait(bank)
                c.te.wait_ge(c.vS.h, xs_done)
                first = True
                for comp in ("r", "i"):
                    for pc in range(2):
                        mm = c.te.matmul(pb[bank][:, 0:512],
                                         CT[comp + d_][:, pc, mc * 128:(mc + 1) * 128],
                                         xsb[comp + d_][:, pc, tsl],
                                         start=first, stop=False)
                        first = False
                mm = c.te.matmul(pb[bank][:, 0:512], diagD[d_][:, mc, :],
                                 resid[:, mc, tsl], start=False, stop=True)
                tick(mm, c.tS)
                c.sc.wait_ge(c.tS.h, c.tS.n)
                if dnum == 0:
                    c.sc.wait_ge(c.vS.h, sf_guard)
                    tick(c.sc.activation(sfv[:, mc, :], pb[bank][:, 0:512], AF.Gelu),
                         c.sS)
                    c.bankg[bank] = (c.sS, c.sS.n)
                else:
                    c.sc.wait_ge(c.vS.h, c.vS.n)
                    tick(c.sc.activation(sbc3[:], pb[bank][:, 0:512], AF.Gelu), c.sS)
                    c.bankg[bank] = (c.sS, c.sS.n)
                    c.ve.wait_ge(c.sS.h, c.sS.n)
                    tick(c.ve.tensor_add(sfv[:, mc, :], sfv[:, mc, :], sbc3[:]), c.vS)
                    tick(c.ve.scalar_tensor_tensor(sfv[:, mc, :], resid[:, mc, tsl],
                                                   1.0, sfv[:, mc, :], ALU.mult,
                                                   ALU.add), c.vS)
        src = [sfv[:, j, :] for j in range(8)]
        nb = [nb3b[:, j, :] for j in range(8)]
        outs = [resid[:, j, tsl] for j in range(8)]
        emit_ln(c, src, nb, onesb_ap, (6, 7), mbc3[:], sbc3[:], 512, out_aps=outs)
        sf_guard = c.vS.n

    if upto in ("l1a", "l1adbg"):
        _emit_debug_out(c, resid, out_t, l1a, nc)
        return _finish(c, nc)

    l1a.close()
    l1f = ExitStack()
    E3 = l1f.enter_context
    wsl = E3(nc.sbuf_tensor("wsl", [128, 4, 8, D], BF16))
    outq = E3(nc.sbuf_tensor("outq", [128, 8, 1024], F32))
    hbuf = E3(nc.sbuf_tensor("hbuf", [128, 3, 8, 512], BF16))
    hbuf_guard = [0, 0, 0]
    wbc = E3(nc.sbuf_tensor("wbc", [128, 8, 1024], BF16))
    rt = E3(nc.sbuf_tensor("rt", [128, 8, 8], F32))
    rr1 = E3(nc.sbuf_tensor("rr1", [128, 8, 8], F32))
    rr2 = E3(nc.sbuf_tensor("rr2", [128, 8, 8], F32))
    rr3 = E3(nc.sbuf_tensor("rr3", [128, 8, 1], F32))
    rr3m = E3(nc.sbuf_tensor("rr3m", [128, 8, 1], U8))
    rr4 = E3(nc.sbuf_tensor("rr4", [128, 8, 1], F32))
    st1 = E3(nc.sbuf_tensor("st1", [128, 512], F32))
    st1_guard = [0, 0]
    nb4 = None
    itmp = E3(nc.sbuf_tensor("itmp", [128, 1], F32))
    hbflat = hbuf[:].rearrange("p a b t -> p (a b t)")
    wT8 = hbflat[0:8, 2 * 4096:2 * 4096 + 1024]
    sb_bcast8 = hbflat[0:8, 2 * 4096 + 1024:2 * 4096 + 2048]
    dWb8 = Sem(nc, "dWb8")
    mbc4 = hbuf[:, 2, :, :].rearrange("p a b -> p (a b)").bitcast(F32)[:, 512:1024]
    ytm = hbuf[:, 1, :, :].rearrange("p a b -> p (a b)").bitcast(F32) \
        .rearrange("p (a b) -> p a b", a=2)
    c.fence()

    c.ve.wait_ge(dWs.h, dWs.n)
    tick(c.ve.reciprocal(itmp[:], sp["tempc"][:]), c.vS)

    slot_guard = [0, 0, 0, 0]
    slot_ready = [None] * 4
    state = {"nslot": 0}
    pending = [(e, m) for _q in range(4) for e in range(8) for m in range(3)]
    slq = []

    def load_one():
        em = pending.pop(0)
        sl = state["nslot"] % 4
        state["nslot"] += 1
        if slot_guard[sl]:
            c.sy.wait_ge(c.tS.h, slot_guard[sl])
        c.sy.wait_ge(dGe.h, 16 * 24)
        tick(c.sy.dma_start(out=wsl[:, sl, :, :],
                            in_=wbf_h[em[0], em[1]].rearrange("(kc p) i -> p kc i",
                                                              p=128)),
             dWsl[sl], 16)
        slot_ready[sl] = (dWsl[sl], dWsl[sl].n)
        slq.append((em, sl))

    def take_load(e, m):
        em, sl = slq.pop(0)
        assert em == (e, m), (em, e, m)
        return sl

    def refill(n=3):
        for _ in range(n):
            if pending:
                load_one()

    refill(3)

    for q in range(4):
        c.sy.wait_ge(c.tS.h, c.tS.n)
        tick(c.sy.dma_start(out=sb_bcast8, in_=bcast8), dWb8, 16)
        c.bank_wait(6)
        c.te.wait_ge(dGr.h, 16)
        for tm in range(8):
            for kc in range(8):
                mm = c.te.matmul(pb[6][:, tm * 8:(tm + 1) * 8],
                                 resid[:, kc,
                                       q * 1024 + tm * 128:q * 1024 + (tm + 1) * 128],
                                 wrB[:, kc, :], start=(kc == 0), stop=(kc == 7))
        tick(mm, c.tS)
        c.ve.wait_ge(c.tS.h, c.tS.n)
        tick(c.ve.tensor_copy(rt[:].rearrange("p a b -> p (a b)"), pb[6][:, 0:64]),
             c.vS)
        c.bankg[6] = (c.vS, c.vS.n)
        v = c.ve
        rta, r1a, r2a = rt[:], rr1[:], rr2[:]
        c.sc.wait_ge(c.vS.h, c.vS.n)
        tick(c.sc.activation(rta, rta, AF.Sigmoid, scale=itmp[:, 0:1]), c.sS)
        c.ve.wait_ge(c.sS.h, c.sS.n)
        sv(c, v.tensor_scalar(r1a, rta, GOLD_L, None, ALU.is_ge))
        sv(c, v.tensor_scalar(r2a, rta, GOLD_U, None, ALU.is_le))
        sv(c, v.tensor_mul(r1a, r1a, r2a))
        sv(c, v.tensor_scalar_add(rta, rta, -GOLD_C))
        c.sc.wait_ge(c.vS.h, c.vS.n)
        tick(c.sc.activation(rta, rta, AF.Abs), c.sS)
        c.ve.wait_ge(c.sS.h, c.sS.n)
        c.sc.wait_ge(c.vS.h, c.vS.n)
        tick(c.sc.activation(r2a, rta, AF.Exp, scale=np.float32(-10.0)), c.sS)
        tick(c.sc.activation(rta, rta, AF.Exp, scale=np.float32(-1.0 / 0.3)), c.sS)
        c.ve.wait_ge(c.sS.h, c.sS.n)
        sv(c, v.tensor_mul(r1a, r1a, r2a))
        sv(c, v.reduce_sum(rr3[:], r1a, axis=AXX))
        sv(c, v.tensor_scalar(rr3m[:], rr3[:], 1e-8, None, ALU.is_lt))
        sv(c, v.reduce_max(rr4[:], rta, axis=AXX))
        for tm in range(8):
            sv(c, v.tensor_scalar(rr2[:, tm, :], rt[:, tm, :], rr4[:, tm, 0:1],
                                 None, ALU.is_equal))
        sv(c, v.scalar_tensor_tensor(r2a, r2a, -1e30, rta, ALU.mult, ALU.add))
        sv(c, v.reduce_max(rr4[:], r2a, axis=AXX))
        for tm in range(8):
            sv(c, v.tensor_scalar(rr2[:, tm, :], rt[:, tm, :], rr4[:, tm, 0:1],
                                 None, ALU.is_ge))
        sv(c, v.tensor_mul(r2a, r2a, rta))
        sv(c, v.reduce_sum(rr4[:], r2a, axis=AXX))
        sv(c, v.tensor_scalar_max(rr4[:], rr4[:], 1e-8))
        sv(c, v.reciprocal(rr4[:], rr4[:]))
        for tm in range(8):
            sv(c, v.tensor_scalar(rr2[:, tm, :], rr2[:, tm, :], rr4[:, tm, 0:1],
                                 None, ALU.mult))
            sv(c, v.select(rr1[:, tm, :],
                          rr3m[:, tm, 0:1].broadcast_to([128, 8]),
                          rr2[:, tm, :], rr1[:, tm, :]))
        sv(c, v.reduce_sum(rr4[:], r1a, axis=AXX))
        sv(c, v.tensor_scalar_max(rr4[:], rr4[:], 1e-8))
        sv(c, v.reciprocal(rr4[:], rr4[:]))
        for tm in range(8):
            sv(c, v.tensor_scalar(rr1[:, tm, :], rr1[:, tm, :], rr4[:, tm, 0:1],
                                 None, ALU.mult))
        vS_w = c.vS.n
        for tm in range(8):
            bank = 6 + (tm % 2)
            c.bank_wait(bank)
            c.te.wait_ge(c.vS.h, vS_w)
            tick(c.te.transpose(pb[bank][0:8, 0:128], rr1[:, tm, :], sb_identf[:]),
                 c.tS)
            c.ve.wait_ge(c.tS.h, c.tS.n)
            tick(v.tensor_copy(wT8[:, tm * 128:(tm + 1) * 128], pb[bank][0:8, 0:128]),
                 c.vS)
            c.bankg[bank] = (c.vS, c.vS.n)
        vS_wt8 = c.vS.n
        c.te.wait_ge(dWb8.h, dWb8.n)
        for e in range(8):
            for half in range(2):
                bank = 6 + half
                c.bank_wait(bank)
                c.te.wait_ge(c.vS.h, vS_wt8)
                tick(c.te.matmul(pb[bank][:, 0:512],
                                 sb_bcast8[:, e * 128:(e + 1) * 128],
                                 wT8[:, half * 512:(half + 1) * 512],
                                 start=True, stop=True), c.tS)
                c.ve.wait_ge(c.tS.h, c.tS.n)
                tick(v.tensor_copy(wbc[:, e, half * 512:(half + 1) * 512],
                                   pb[bank][:, 0:512]), c.vS)
                c.bankg[bank] = (c.vS, c.vS.n)

        c.ve.wait_ge(dOy[0].h, dOy[0].n)
        c.ve.wait_ge(dOy[1].h, dOy[1].n)
        def emit_wd(jobs):
            tt2_ = jobs[0][4]
            toff_ = q * 1024 + tt2_ * 512
            tsl_ = slice(toff_, toff_ + 512)
            for dc in range(8):
                bank = 4 + (dc % 2)
                c.bank_wait(bank)
                first = True
                for (e_, hb_, sld_, vS_h_, _t) in jobs:
                    c.te.wait_ge(c.vS.h, vS_h_)
                    c.te.wait_ge(slot_ready[sld_][0].h, slot_ready[sld_][1])
                    for ic in range(8):
                        mm = c.te.matmul(pb[bank][:, 0:512],
                                         wsl[:, sld_, ic, dc * 128:(dc + 1) * 128],
                                         hbuf[:, hb_, ic, :],
                                         start=(first and ic == 0),
                                         stop=(e_ == jobs[-1][0] and ic == 7))
                        first = False
                    tick(mm, c.tS)
                    slot_guard[sld_] = c.tS.n
                c.ve.wait_ge(c.tS.h, c.tS.n)
                od = outq[:, dc, tt2_ * 512:(tt2_ + 1) * 512]
                if jobs[0][0] == 0:
                    tick(c.ve.scalar_tensor_tensor(od, resid[:, dc, tsl_], 1.0,
                                                   pb[bank][:, 0:512], ALU.mult,
                                                   ALU.add), c.vS)
                else:
                    tick(c.ve.tensor_add(od, od, pb[bank][:, 0:512]), c.vS)
                c.bankg[bank] = (c.vS, c.vS.n)
            for (e_, hb_, sld_, vS_h_, _t) in jobs:
                hbuf_guard[hb_] = max(hbuf_guard[hb_], c.tS.n)

        pend = {0: [], 1: []}
        for e in range(8):
            slg = take_load(e, 0)
            slu = take_load(e, 1)
            sld = take_load(e, 2)
            for tt2 in range(2):
                toff = q * 1024 + tt2 * 512
                tsl = slice(toff, toff + 512)
                hb = (e * 2 + tt2) % 3
                for t2 in (0, 1):
                    if any(j[1] == hb for j in pend[t2]):
                        emit_wd(pend[t2])
                        pend[t2] = []
                for ic in range(8):
                    bg = ic % 2
                    bu_ = 2 + (ic % 2)
                    c.bank_wait(bg)
                    c.te.wait_ge(slot_ready[slg][0].h, slot_ready[slg][1])
                    for kc in range(8):
                        mm = c.te.matmul(pb[bg][:, 0:512],
                                         wsl[:, slg, kc, ic * 128:(ic + 1) * 128],
                                         resid[:, kc, tsl], start=(kc == 0),
                                         stop=(kc == 7))
                    tick(mm, c.tS)
                    slot_guard[slg] = c.tS.n
                    t_g = c.tS.n
                    c.bank_wait(bu_)
                    c.te.wait_ge(slot_ready[slu][0].h, slot_ready[slu][1])
                    for kc in range(8):
                        mm = c.te.matmul(pb[bu_][:, 0:512],
                                         wsl[:, slu, kc, ic * 128:(ic + 1) * 128],
                                         resid[:, kc, tsl], start=(kc == 0),
                                         stop=(kc == 7))
                    tick(mm, c.tS)
                    slot_guard[slu] = c.tS.n
                    t_u = c.tS.n
                    c.sc.wait_ge(c.tS.h, t_g)
                    c.sc.wait_ge(c.vS.h, st1_guard[0])
                    tick(c.sc.activation(st1[:], pb[bg][:, 0:512], AF.Silu), c.sS)
                    c.bankg[bg] = (c.sS, c.sS.n)
                    c.ve.wait_ge(c.sS.h, c.sS.n)
                    c.ve.wait_ge(c.tS.h, max(t_u, hbuf_guard[hb]))
                    tick(c.ve.tensor_mul(st1[:], st1[:], pb[bu_][:, 0:512]), c.vS)
                    c.bankg[bu_] = (c.vS, c.vS.n)
                    tick(c.ve.tensor_mul(hbuf[:, hb, ic, :], st1[:],
                                         wbc[:, e, tt2 * 512:(tt2 + 1) * 512]), c.vS)
                    st1_guard[0] = c.vS.n
                vS_h = c.vS.n
                pend[tt2].append((e, hb, sld, vS_h, tt2))
            for t2 in (0, 1):
                if len(pend[t2]) == 2:
                    emit_wd(pend[t2])
                    pend[t2] = []
            refill(3)
        for tt2 in (0, 1):
            if pend[tt2]:
                emit_wd(pend[tt2])
                pend[tt2] = []

        for tt2 in range(2):
            toff = q * 1024 + tt2 * 512
            src = [outq[:, j, tt2 * 512:(tt2 + 1) * 512] for j in range(8)]
            nb = [hbuf[:, 0, j, :] for j in range(8)]
            emit_ln(c, src, nb, onesb_ap, (6, 7), mbc4, st1[:], 512,
                    out_aps=src)
            st1_guard[0] = c.vS.n
            hbuf_guard[0] = c.tS.n
            vS_y = c.vS.n
            for tm in range(4):
                ybuf = tm % 2
                for dc in range(8):
                    bank = tm % 2
                    c.bank_wait(bank)
                    c.te.wait_ge(c.vS.h, vS_y)
                    tick(c.te.transpose(pb[bank][:, 0:128],
                                        outq[:, dc,
                                             tt2 * 512 + tm * 128:tt2 * 512 + (tm + 1) * 128],
                                        sb_identf[:]), c.tS)
                    c.sc.wait_ge(c.tS.h, c.tS.n)
                    c.sc.wait_ge(dOy[ybuf].h, dOy[ybuf].n)
                    tick(c.sc.activation(ytm[:, ybuf, dc * 128:(dc + 1) * 128],
                                         pb[bank][:, 0:128], AF.Copy), c.sS)
                    c.bankg[bank] = (c.sS, c.sS.n)
                c.sc.wait_ge(c.sS.h, c.sS.n)
                tick(c.sc.dma_start(out=out_t[toff + tm * 128:toff + (tm + 1) * 128, :],
                                    in_=ytm[:, ybuf, :]), dOy[ybuf], 16)

    c.sc.wait_ge(dOy[0].h, dOy[0].n)
    c.sc.wait_ge(dOy[1].h, dOy[1].n)
    return _finish(c, nc)


def _emit_debug_out(c, resid, out_t, scope, nc):
    dbg = scope.enter_context(nc.sbuf_tensor("dbg", [128, 1024], F32))
    flat = out_t.flatten()
    c.ve.wait_ge(c.dWx.h, 128)
    for ch in range(8):
        for qq in range(4):
            c.ve.wait_ge(c.sS.h, c.sS.n)
            c.ve.wait_ge(c.dO.h, c.dO.n)
            tick(c.ve.tensor_copy(dbg[:], resid[:, ch, qq * 1024:(qq + 1) * 1024]),
                 c.vS)
            c.sc.wait_ge(c.vS.h, c.vS.n)
            off = ch * 128 * T + qq * 1024
            tick(c.sc.dma_start(out=flat[off:off + 128 * T // 4]
                                .rearrange("(p t) -> p t", p=128, t=1024)
                                if False else
                                flat[ch * 128 * T:(ch + 1) * 128 * T]
                                .rearrange("(p t) -> p t", p=128)
                                [:, qq * 1024:(qq + 1) * 1024],
                                in_=dbg[:]), c.dO, 16)
    c.sc.wait_ge(c.dO.h, c.dO.n)


def _finish(c, nc):
    c.run()
    return nc


def _perm_planar():
    perm = np.zeros(D, dtype=np.int64)
    for cc in range(4):
        for qh in range(2):
            for j in range(128):
                perm[(cc * 2 + qh) * 128 + j] = 4 * (qh * 128 + j) + cc
    return perm


def _cols(vec, ncol):
    return np.ascontiguousarray(np.asarray(vec, dtype=np.float32)
                                .reshape(ncol, 128).T)


def prepare_in_maps(inp):
    f32 = lambda a: np.ascontiguousarray(np.asarray(a, dtype=np.float32))
    perm = _perm_planar()
    sh = {}
    sh["wk"] = np.ascontiguousarray(f32(inp["L0_attn_Wk"])[perm, :].T)
    sh["bk"] = _cols(f32(inp["L0_attn_bk"])[perm], 8)
    sh["wv"] = np.ascontiguousarray(f32(inp["L0_attn_Wv"])[perm, :].T)
    sh["bv"] = _cols(f32(inp["L0_attn_bv"])[perm], 8)
    sh["lng"] = _cols(f32(inp["L0_attn_ln_g"])[perm], 8)
    sh["lnb"] = _cols(f32(inp["L0_attn_ln_b"])[perm], 8)
    sh["wo"] = np.ascontiguousarray(f32(inp["L0_attn_Wo"])[:, perm].T)
    sh["bo"] = _cols(f32(inp["L0_attn_bo"]), 8)
    sh["win"] = f32(inp["L0_ffn_Win"])
    sh["binc"] = np.concatenate([f32(inp["L0_ffn_bin_"])] * 2).reshape(128, 1)
    sh["w1d"] = np.vstack([f32(inp["L0_ffn_W1"])] * 2)
    sh["b1d"] = np.vstack([f32(inp["L0_ffn_b1"])] * 2)
    sh["w2d"] = np.vstack([f32(inp["L0_ffn_W2"])] * 2)
    sh["b2d"] = np.concatenate([f32(inp["L0_ffn_b2"])] * 2).reshape(128, 1)
    sh["woutd"] = np.vstack([f32(inp["L0_ffn_Wout"])] * 2)
    sh["boutc"] = _cols(f32(inp["L0_ffn_bout"]), 8)
    for d_ in ("f", "b"):
        pf = "L1_attn_fwd_" if d_ == "f" else "L1_attn_bwd_"
        sh["lr" + d_] = _cols(f32(inp[pf + "log_real"]), 2)
        sh["im" + d_] = _cols(f32(inp[pf + "imag"]), 2)
        sh["ld" + d_] = _cols(f32(inp[pf + "log_Delta"]), 2)
        sh["br" + d_] = f32(inp[pf + "B_real"])
        sh["bi" + d_] = f32(inp[pf + "B_imag"])
        sh["cr" + d_] = np.ascontiguousarray(f32(inp[pf + "C_real"]).T)
        sh["ci" + d_] = np.ascontiguousarray(f32(inp[pf + "C_imag"]).T)
        sh["dc" + d_] = _cols(f32(inp[pf + "D"]), 8)
    sh["wg"] = f32(inp["L1_ffn_Wg"])
    sh["wu"] = f32(inp["L1_ffn_Wu"])
    sh["wd"] = f32(inp["L1_ffn_Wd"])
    sh["wr"] = f32(inp["L1_ffn_Wr"])
    sh["tempc"] = np.repeat(f32(inp["L1_ffn_temperature"]), 128).reshape(128, 1)
    sh["identf"] = np.eye(128, dtype=np.float32)
    sh["identb"] = np.eye(128, dtype=ml_dtypes.bfloat16)
    sh["onesb"] = np.ones((128, 128), dtype=ml_dtypes.bfloat16)
    b8 = np.zeros((8, D), dtype=ml_dtypes.bfloat16)
    for e in range(8):
        b8[e, e * 128:(e + 1) * 128] = 1
    sh["bcast8"] = b8

    x_full = f32(inp["L0_attn_x"])
    return [dict(sh, x=np.ascontiguousarray(x_full[core])) for core in range(NCORE)]


_NC_CACHE = {}


def kernel(**inp):
    in_maps = prepare_in_maps(inp)
    if "all" not in _NC_CACHE:
        _NC_CACHE["all"] = build_program("all")
    nc = _NC_CACHE["all"]
    res = run_bass_kernel_spmd(nc, in_maps, core_ids=list(range(NCORE)))
    return np.stack([res.results[i]["out"] for i in range(NCORE)], axis=0)


# version 09
# speedup vs baseline: 1.0799x; 1.0799x over previous
import math
import numpy as np
import ml_dtypes
import concourse.bass as bass
import concourse.mybir as mybir
from concourse.bass_utils import run_bass_kernel_spmd

F32 = mybir.dt.float32
U8 = mybir.dt.uint8
BF16 = mybir.dt.bfloat16
AF = mybir.ActivationFunctionType
ALU = mybir.AluOpType
AXX = mybir.AxisListType.X

T = 4096
D = 1024
NCORE = 8
GL = 512
NG = T // GL
P = 256
TWOPI = 6.283185307179586
PI_CLAMP = 3.14159260
MAGIC = 12582912.0
GOLD_C = 1.0 / math.e
GOLD_L = 0.5 - math.log(4.0 / 3.0)
GOLD_U = 0.5
LN_EPS = 1e-5


class _RI:
    __slots__ = ("p",)
    def __init__(self, p):
        self.p = p

    def then_inc(self, sem_h, v):
        self.p.append((sem_h, v))
        return self


class Rec:
    def __init__(self):
        self.ops = []

    def op(self, name, *a, **k):
        pend = []
        self.ops.append((name, a, k, pend))
        return _RI(pend)

    def __getattr__(self, name):
        if name.startswith("_") or name == "ops":
            raise AttributeError(name)
        def call(*a, **k):
            return self.op(name, *a, **k)
        return call

    def replay(self, eng):
        for name, a, k, pend in self.ops:
            ins = getattr(eng, name)(*a, **k)
            for sem_h, v in pend:
                ins = ins.then_inc(sem_h, v)


class Sem:
    def __init__(self, nc, name):
        self.h = nc.alloc_semaphore(name)
        self.n = 0


def tick(instr, sem, k=1):
    instr.then_inc(sem.h, k)
    sem.n += k


def sv(c, instr):
    tick(instr, c.vS)
    c.ve.wait_ge(c.vS.h, c.vS.n)


class C:
    def __init__(self, nc):
        self.nc = nc
        self.sy = Rec()
        self.sc = Rec()
        self.ve = Rec()
        self.te = Rec()
        self.gp = Rec()
        self.dW = Sem(nc, "dW")
        self.dO = Sem(nc, "dO")
        self.dG = Sem(nc, "dG")
        self.tS = Sem(nc, "tS")
        self.vS = Sem(nc, "vS")
        self.sS = Sem(nc, "sS")
        self.gS = Sem(nc, "gS")
        self.bankg = [(None, 0)] * 8

    def bank_wait(self, i):
        s, n = self.bankg[i]
        if s is not None and n > 0:
            self.te.wait_ge(s.h, n)

    def fence(self, engines=None):
        for rec_ in engines or (self.sy, self.sc, self.ve, self.te):
            rec_.wait_ge(self.vS.h, self.vS.n)
            rec_.wait_ge(self.sS.h, self.sS.n)
            rec_.wait_ge(self.tS.h, self.tS.n)

    def run(self):
        with self.nc.Block() as block:
            block.sync(lambda e: self.sy.replay(e))
            block.scalar(lambda e: self.sc.replay(e))
            block.vector(lambda e: self.ve.replay(e))
            block.tensor(lambda e: self.te.replay(e))
            block.gpsimd(lambda e: self.gp.replay(e))


def emit_ln(c, src_aps, nbf_aps, onesb, banks, mbc, sbc, W,
            out_aps=None, gcols=None, bcols=None, tq=None, filler=None):
    pb = c.pb
    bs, bq = banks
    v_src = c.vS.n
    cast_cnt = []
    for j in range(8):
        c.sc.wait_ge(c.vS.h, v_src)
        tick(c.sc.activation(nbf_aps[j], src_aps[j], AF.Copy), c.sS)
        cast_cnt.append(c.sS.n)
    c.bank_wait(bs)
    for j in range(8):
        c.te.wait_ge(c.sS.h, cast_cnt[j])
        mm = c.te.matmul(pb[bs][:, 0:W], onesb, nbf_aps[j],
                         start=(j == 0), stop=(j == 7))
    tick(mm, c.tS)
    t_sum = c.tS.n
    c.sc.wait_ge(c.tS.h, t_sum)
    tick(c.sc.activation(mbc, pb[bs][:, 0:W], AF.Identity,
                         scale=np.float32(1.0 / 1024.0)), c.sS)
    c.bankg[bs] = (c.sS, c.sS.n)
    c.ve.wait_ge(c.sS.h, c.sS.n)
    for j in range(8):
        tick(c.ve.tensor_sub(src_aps[j], src_aps[j], mbc), c.vS)
    v_center = c.vS.n
    if filler is not None:
        filler()
        filler = None
    sq_cnt = []
    for j in range(8):
        c.sc.wait_ge(c.tS.h, t_sum)
        c.sc.wait_ge(c.vS.h, v_center)
        tick(c.sc.activation(nbf_aps[j], src_aps[j], AF.Square), c.sS)
        sq_cnt.append(c.sS.n)
    c.bank_wait(bq)
    for j in range(8):
        c.te.wait_ge(c.sS.h, sq_cnt[j])
        mm = c.te.matmul(pb[bq][:, 0:W], onesb, nbf_aps[j],
                         start=(j == 0), stop=(j == 7))
    tick(mm, c.tS)
    t_sq = c.tS.n
    c.sc.wait_ge(c.tS.h, t_sq)
    tick(c.sc.activation(sbc, pb[bq][:, 0:W], AF.Sqrt, scale=np.float32(1.0 / 1024.0),
                         bias=np.float32(LN_EPS)), c.sS)
    c.bankg[bq] = (c.sS, c.sS.n)
    c.ve.wait_ge(c.sS.h, c.sS.n)
    tick(c.ve.reciprocal(mbc, sbc), c.vS)
    fin = []
    if out_aps is not None:
        for j in range(8):
            if gcols is None:
                tick(c.ve.tensor_mul(out_aps[j], src_aps[j], mbc), c.vS)
                fin.append(c.vS.n)
            else:
                c.ve.wait_ge(c.sS.h, c.sS.n)
                tick(c.ve.tensor_mul(tq, src_aps[j], mbc), c.vS)
                c.sc.wait_ge(c.vS.h, c.vS.n)
                tick(c.sc.activation(out_aps[j], tq, AF.Identity,
                                     scale=gcols[j], bias=bcols[j]), c.sS)
                fin.append(c.sS.n)
    return fin


def emit_cx_scan(c, xr, xi, tabs, cd, reverse):
    powAr, powAi, powAiN, p16r, p16i, p16iN = tabs
    v = c.ve

    def A1(k):
        return (powAr[:, cd, k - 1:k], powAi[:, cd, k - 1:k],
                powAiN[:, cd, k - 1:k])

    def A16(k):
        return (p16r[:, cd, k - 1:k], p16i[:, cd, k - 1:k],
                p16iN[:, cd, k - 1:k])

    def mac(out_r, out_i, in_r, in_i, A):
        ar, ai, ain = A
        tick(v.scalar_tensor_tensor(out_r, in_r, ar, out_r, ALU.mult, ALU.add), c.vS)
        tick(v.scalar_tensor_tensor(out_r, in_i, ain, out_r, ALU.mult, ALU.add), c.vS)
        tick(v.scalar_tensor_tensor(out_i, in_i, ar, out_i, ALU.mult, ALU.add), c.vS)
        tick(v.scalar_tensor_tensor(out_i, in_r, ai, out_i, ALU.mult, ALU.add), c.vS)

    v16r = xr.rearrange("p (k m) -> p k m", m=16)
    v16i = xi.rearrange("p (k m) -> p k m", m=16)
    vgr = xr.rearrange("p (g r m) -> p g r m", m=16, r=16)
    vgi = xi.rearrange("p (g r m) -> p g r m", m=16, r=16)
    vfr = xr.rearrange("p (g q) -> p g q", q=256)
    vfi = xi.rearrange("p (g q) -> p g q", q=256)

    if not reverse:
        for t in range(1, 16):
            mac(v16r[:, :, t], v16i[:, :, t],
                v16r[:, :, t - 1], v16i[:, :, t - 1], A1(1))
        for r in range(1, 16):
            mac(vgr[:, :, r, 15], vgi[:, :, r, 15],
                vgr[:, :, r - 1, 15], vgi[:, :, r - 1, 15], A16(1))
        for g in range(1, 16):
            mac(vfr[:, g, 255:256], vfi[:, g, 255:256],
                vfr[:, g - 1, 255:256], vfi[:, g - 1, 255:256], A16(16))
        for r in range(0, 15):
            mac(vgr[:, 1:16, r, 15], vgi[:, 1:16, r, 15],
                vfr[:, 0:15, 255], vfi[:, 0:15, 255], A16(r + 1))
        for t in range(0, 15):
            mac(v16r[:, 1:256, t], v16i[:, 1:256, t],
                v16r[:, 0:255, 15], v16i[:, 0:255, 15], A1(t + 1))
    else:
        for s in range(1, 16):
            mac(v16r[:, :, 15 - s], v16i[:, :, 15 - s],
                v16r[:, :, 16 - s], v16i[:, :, 16 - s], A1(1))
        for s in range(1, 16):
            mac(vgr[:, :, 15 - s, 0], vgi[:, :, 15 - s, 0],
                vgr[:, :, 16 - s, 0], vgi[:, :, 16 - s, 0], A16(1))
        for s in range(1, 16):
            mac(vfr[:, 15 - s, 0:1], vfi[:, 15 - s, 0:1],
                vfr[:, 16 - s, 0:1], vfi[:, 16 - s, 0:1], A16(16))
        for r in range(1, 16):
            mac(vgr[:, 0:15, r, 0], vgi[:, 0:15, r, 0],
                vfr[:, 1:16, 0], vfi[:, 1:16, 0], A16(16 - r))
        for t in range(1, 16):
            mac(v16r[:, 0:255, t], v16i[:, 0:255, t],
                v16r[:, 1:256, 0], v16i[:, 1:256, 0], A1(16 - t))


def emit_cx_mul(c, or_, oi, ar, ai, br, bi, t1, t2):
    g = c.gp
    tick(g.tensor_mul(t1, ar, br), c.gS)
    tick(g.tensor_mul(t2, ai, bi), c.gS)
    tick(g.tensor_sub(or_, t1, t2), c.gS)
    tick(g.tensor_mul(t1, ar, bi), c.gS)
    tick(g.tensor_mul(t2, ai, br), c.gS)
    tick(g.tensor_add(oi, t1, t2), c.gS)


def build_program(upto="all"):
    nc = bass.Bass("TRN2", target_bir_lowering=False,
                   detect_race_conditions=False)
    c = C(nc)
    from contextlib import ExitStack
    root = ExitStack()

    def din(name, shape, dtype=F32):
        return nc.dram_tensor(name, shape, dtype, kind="ExternalInput").ap()

    x_in = din("x", [T, D])
    wk = din("wk", [D, D]); bk = din("bk", [128, 8])
    wv = din("wv", [D, D]); bv = din("bv", [128, 8])
    lng = din("lng", [128, 8]); lnb = din("lnb", [128, 8])
    wo = din("wo", [D, D]); bo = din("bo", [128, 8])
    win = din("win", [D, 64]); binc = din("binc", [128, 1])
    w1d = din("w1d", [128, 16]); b1d = din("b1d", [128, 16])
    w2d = din("w2d", [128, 16]); b2d = din("b2d", [128, 1])
    woutd = din("woutd", [128, D]); boutc = din("boutc", [128, 8])
    lrc = {}; imc = {}; ldc = {}; brr = {}; bii = {}; crr = {}; cii = {}; dcol = {}
    for d_ in ("f", "b"):
        lrc[d_] = din("lr" + d_, [128, 2]); imc[d_] = din("im" + d_, [128, 2])
        ldc[d_] = din("ld" + d_, [128, 2])
        brr[d_] = din("br" + d_, [P, D]); bii[d_] = din("bi" + d_, [P, D])
        crr[d_] = din("cr" + d_, [P, D]); cii[d_] = din("ci" + d_, [P, D])
        dcol[d_] = din("dc" + d_, [128, 8])
    wg = din("wg", [8, D, D]); wu = din("wu", [8, D, D]); wd = din("wd", [8, D, D])
    wr = din("wr", [D, 8]); tempc = din("tempc", [128, 1])
    identf = din("identf", [128, 128], F32)
    identb = din("identb", [128, 128], BF16)
    onesb_in = din("onesb", [128, 128], BF16)
    bcast8 = din("bcast8", [8, D], BF16)

    out_t = nc.dram_tensor("out", [T, D], F32, kind="ExternalOutput").ap()

    xbf_h = nc.dram_tensor("xbf_h", [T, D], BF16, kind="Internal").ap()
    wbf_h = nc.dram_tensor("wbf_h", [8, 3, D, D], BF16, kind="Internal").ap()

    ES = root.enter_context
    resid = ES(nc.sbuf_tensor("resid", [128, 8, T], BF16))
    sb_identf = ES(nc.sbuf_tensor("sb_identf", [128, 128], F32))
    sb_identb = ES(nc.sbuf_tensor("sb_identb", [128, 128], BF16))
    sb_onesb = ES(nc.sbuf_tensor("sb_onesb", [128, 128], BF16))
    sp = {}
    for nm in ("bk", "bv", "lng", "lnb", "bo", "boutc", "dcf", "dcb"):
        sp[nm] = ES(nc.sbuf_tensor("sp_" + nm, [128, 8], F32))
    for nm in ("binc", "b2d", "tempc"):
        sp[nm] = ES(nc.sbuf_tensor("sp_" + nm, [128, 1], F32))
    for nm in ("w1d", "b1d", "w2d"):
        sp[nm] = ES(nc.sbuf_tensor("sp_" + nm, [128, 16], F32))
    for nm in ("lrf", "imf", "ldf", "lrb", "imb", "ldb"):
        sp[nm] = ES(nc.sbuf_tensor("sp_" + nm, [128, 2], F32))
    wrB = ES(nc.sbuf_tensor("wrB", [128, 8, 8], BF16))
    powAr = ES(nc.sbuf_tensor("powAr", [128, 4, 16], F32))
    powAi = ES(nc.sbuf_tensor("powAi", [128, 4, 16], F32))
    powAiN = ES(nc.sbuf_tensor("powAiN", [128, 4, 16], F32))
    p16r = ES(nc.sbuf_tensor("p16r", [128, 4, 16], F32))
    p16i = ES(nc.sbuf_tensor("p16i", [128, 4, 16], F32))
    p16iN = ES(nc.sbuf_tensor("p16iN", [128, 4, 16], F32))
    sBs = ES(nc.sbuf_tensor("sBs", [128, 4, 3], F32))
    ttc1 = ES(nc.sbuf_tensor("ttc1", [128, 2], F32))
    ttc2 = ES(nc.sbuf_tensor("ttc2", [128, 2], F32))
    ttc3 = ES(nc.sbuf_tensor("ttc3", [128, 2], F32))
    ttc4 = ES(nc.sbuf_tensor("ttc4", [128, 2], F32))
    cst = ES(nc.sbuf_tensor("cst", [128, 2, 4], F32))
    ttcm = ES(nc.sbuf_tensor("ttcm", [128, 2], U8))
    ttg1 = ES(nc.sbuf_tensor("ttg1", [128, 2], F32))
    ttg2 = ES(nc.sbuf_tensor("ttg2", [128, 2], F32))
    zz2 = ES(nc.sbuf_tensor("zz2", [128, 2, 16], F32))
    tabs = (powAr, powAi, powAiN, p16r, p16i, p16iN)

    c.pb = [ES(nc.psum_tensor(f"pbank{i}", [128, 512], F32)) for i in range(8)]
    pb = c.pb

    dGx = Sem(nc, "dGx"); dGk = Sem(nc, "dGk"); dGv = Sem(nc, "dGv")
    dGo = Sem(nc, "dGo"); dGc = Sem(nc, "dGc"); dGr = Sem(nc, "dGr")
    dGe = Sem(nc, "dGe"); dGi = Sem(nc, "dGi")
    dWs = Sem(nc, "dWs"); dWx = Sem(nc, "dWx"); dWkv = Sem(nc, "dWkv")
    dWwo = Sem(nc, "dWwo"); dWf = Sem(nc, "dWf"); dWct = Sem(nc, "dWct")
    dWb = Sem(nc, "dWb")
    dWsl = [Sem(nc, f"dWsl{i}") for i in range(4)]
    dOy = [Sem(nc, "dOy0"), Sem(nc, "dOy1")]
    tick(c.gp.dma_start(out=xbf_h, in_=x_in), dGx, 16)
    tick(c.gp.dma_start(out=wrB[:], in_=wr.rearrange("(kc p) e -> p kc e", p=128)),
         dGr, 16)
    _emit_expert_casts = lambda: [
        tick(c.gp.dma_start(out=wbf_h[e, m], in_=wsrc[e]), dGe, 16)
        for e in range(8) for m, wsrc in ((0, wg), (1, wu), (2, wd))]

    for nm, src in (("bk", bk), ("bv", bv), ("lng", lng), ("lnb", lnb),
                    ("bo", bo), ("boutc", boutc), ("dcf", dcol["f"]),
                    ("dcb", dcol["b"]), ("binc", binc), ("b2d", b2d),
                    ("tempc", tempc), ("w1d", w1d), ("b1d", b1d), ("w2d", w2d),
                    ("lrf", lrc["f"]), ("imf", imc["f"]), ("ldf", ldc["f"]),
                    ("lrb", lrc["b"]), ("imb", imc["b"]), ("ldb", ldc["b"])):
        tick(c.sy.dma_start(out=sp[nm][:], in_=src), dWs, 16)
    tick(c.sy.dma_start(out=sb_identf[:], in_=identf), dWs, 16)
    tick(c.sy.dma_start(out=sb_identb[:], in_=identb), dWs, 16)
    tick(c.sy.dma_start(out=sb_onesb[:], in_=onesb_in), dWs, 16)

    c.sy.wait_ge(dGx.h, 16)
    for ch in range(8):
        tick(c.sy.dma_start(out=resid[:, ch, :], in_=xbf_h[:, ch * 128:(ch + 1) * 128],
                            transpose=True), dWx, 16)
    c.dWx = dWx

    for rec_ in (c.sc, c.ve, c.te):
        rec_.wait_ge(dWs.h, dWs.n)

    l0 = ExitStack()
    E0 = l0.enter_context
    WkT = E0(nc.sbuf_tensor("WkT", [128, 8, D], BF16))
    WvT = E0(nc.sbuf_tensor("WvT", [128, 8, D], BF16))
    WoT = E0(nc.sbuf_tensor("WoT", [128, 8, D], BF16))
    posrow = E0(nc.sbuf_tensor("posrow", [128, T], BF16))
    kys2 = E0(nc.sbuf_tensor("kys2", [128, 2, 8, GL], F32))
    vrt = E0(nc.sbuf_tensor("vrt", [128, 8, GL], F32))
    bnd = E0(nc.sbuf_tensor("bnd", [128, 8, GL], F32))
    tq1 = E0(nc.sbuf_tensor("tq1", [128, GL], F32))
    tq2 = E0(nc.sbuf_tensor("tq2", [128, GL], F32))
    mbc = E0(nc.sbuf_tensor("mbc", [128, GL], F32))
    sbc = E0(nc.sbuf_tensor("sbc", [128, GL], F32))
    ones512 = E0(nc.sbuf_tensor("ones512", [128, GL], F32))
    kn1 = E0(nc.sbuf_tensor("kn1", [128, GL], F32))
    kn2 = E0(nc.sbuf_tensor("kn2", [128, GL], F32))
    carry = E0(nc.sbuf_tensor("carry", [128, 8], F32))

    tick(c.gp.dma_start(out=WkT[:], in_=wk.rearrange("(kc p) m -> p kc m", p=128)),
         dWkv, 16)
    tick(c.gp.dma_start(out=WvT[:], in_=wv.rearrange("(kc p) m -> p kc m", p=128)),
         dWkv, 16)
    tick(c.gp.dma_start(out=WoT[:], in_=wo.rearrange("(kc p) m -> p kc m", p=128)),
         dWwo, 16)

    c.ve.wait_ge(dWs.h, dWs.n)
    tick(c.ve.memset(zz2[:], 0.0), c.vS)

    def rangered(dst, src_ap, shift):
        v = c.ve
        sv(c, v.tensor_scalar(ttc2[:], src_ap, 1.0 / TWOPI, shift / TWOPI,
                              ALU.mult, ALU.add))
        sv(c, v.tensor_scalar_add(ttc2[:], ttc2[:], MAGIC))
        sv(c, v.tensor_scalar_add(ttc2[:], ttc2[:], -MAGIC))
        sv(c, v.tensor_scalar_mul(ttc2[:], ttc2[:], -TWOPI))
        sv(c, v.tensor_add(ttc2[:], ttc2[:], src_ap))
        sv(c, v.tensor_scalar_add(ttc2[:], ttc2[:], shift))
        sv(c, v.tensor_scalar_min(ttc2[:], ttc2[:], PI_CLAMP))
        sv(c, v.tensor_scalar_max(dst, ttc2[:], -PI_CLAMP))

    for dnum, d_ in enumerate(("f", "b")):
        lr_, im_, ldl = sp["lr" + d_][:], sp["im" + d_][:], sp["ld" + d_][:]
        cdsl = slice(dnum * 2, dnum * 2 + 2)
        v, s = c.ve, c.sc
        c.sc.wait_ge(c.vS.h, c.vS.n)
        tick(s.activation(ttc1[:], ldl, AF.Exp), c.sS)
        tick(s.activation(ttc2[:], lr_, AF.Exp), c.sS)
        c.ve.wait_ge(c.sS.h, c.sS.n)
        sv(c, v.tensor_mul(ttc3[:], ttc2[:], ttc1[:]))
        sv(c, v.tensor_scalar_mul(ttc3[:], ttc3[:], -1.0))
        sv(c, v.tensor_mul(ttc4[:], im_, ttc1[:]))
        c.sc.wait_ge(c.vS.h, c.vS.n)
        tick(s.activation(ttc1[:], ttc3[:], AF.Exp), c.sS)
        c.ve.wait_ge(c.sS.h, c.sS.n)
        rangered(ttc3[:], ttc4[:], 0.0)
        c.sc.wait_ge(c.vS.h, c.vS.n)
        tick(s.activation(ttc3[:], ttc3[:], AF.Sin), c.sS)
        c.ve.wait_ge(c.sS.h, c.sS.n)
        rangered(ttc2[:], ttc4[:], math.pi / 2)
        c.sc.wait_ge(c.vS.h, c.vS.n)
        tick(s.activation(ttc4[:], ttc2[:], AF.Sin), c.sS)
        c.ve.wait_ge(c.sS.h, c.sS.n)
        sv(c, v.tensor_mul(powAr[:, cdsl, 0], ttc1[:], ttc4[:]))
        sv(c, v.tensor_mul(powAi[:, cdsl, 0], ttc1[:], ttc3[:]))
        sv(c, v.tensor_mul(ttc1[:], im_, im_))
        c.sc.wait_ge(c.vS.h, c.vS.n)
        tick(s.activation(ttc2[:], lr_, AF.Exp), c.sS)
        c.ve.wait_ge(c.sS.h, c.sS.n)
        sv(c, v.tensor_mul(ttc3[:], ttc2[:], ttc2[:]))
        sv(c, v.tensor_add(ttc1[:], ttc1[:], ttc3[:]))
        sv(c, v.tensor_scalar(ttcm[:], ttc1[:], 1e-12, None, ALU.is_lt))
        sv(c, v.tensor_scalar_mul(ttc2[:], ttc2[:], -1.0))
        sv(c, v.memset(cst[:, :, 0], 1e-6))
        sv(c, v.memset(cst[:, :, 1], 0.0))
        sv(c, v.select(ttc2[:], ttcm[:], cst[:, :, 0], ttc2[:]))
        sv(c, v.select(ttc4[:], ttcm[:], cst[:, :, 1], im_))
        sv(c, v.tensor_mul(ttc1[:], ttc2[:], ttc2[:]))
        sv(c, v.tensor_mul(ttc3[:], ttc4[:], ttc4[:]))
        sv(c, v.tensor_add(ttc1[:], ttc1[:], ttc3[:]))
        sv(c, v.reciprocal(ttc1[:], ttc1[:]))
        sv(c, v.tensor_scalar_add(ttc3[:], powAr[:, cdsl, 0], -1.0))
        sv(c, v.tensor_mul(cst[:, :, 2], ttc3[:], ttc2[:]))
        sv(c, v.tensor_mul(cst[:, :, 3], powAi[:, cdsl, 0], ttc4[:]))
        sv(c, v.tensor_add(cst[:, :, 2], cst[:, :, 2], cst[:, :, 3]))
        sv(c, v.tensor_mul(sBs[:, cdsl, 0], cst[:, :, 2], ttc1[:]))
        sv(c, v.tensor_mul(cst[:, :, 2], powAi[:, cdsl, 0], ttc2[:]))
        sv(c, v.tensor_mul(cst[:, :, 3], ttc3[:], ttc4[:]))
        sv(c, v.tensor_sub(cst[:, :, 2], cst[:, :, 2], cst[:, :, 3]))
        sv(c, v.tensor_mul(sBs[:, cdsl, 1], cst[:, :, 2], ttc1[:]))
        sv(c, v.tensor_scalar_mul(sBs[:, cdsl, 2], sBs[:, cdsl, 1], -1.0))
        c.gp.wait_ge(c.vS.h, c.vS.n)
        for k in range(1, 16):
            emit_cx_mul(c, powAr[:, cdsl, k], powAi[:, cdsl, k],
                        powAr[:, cdsl, k - 1], powAi[:, cdsl, k - 1],
                        powAr[:, cdsl, 0], powAi[:, cdsl, 0], ttg1[:], ttg2[:])
        tick(c.gp.tensor_copy(p16r[:, cdsl, 0], powAr[:, cdsl, 15]), c.gS)
        tick(c.gp.tensor_copy(p16i[:, cdsl, 0], powAi[:, cdsl, 15]), c.gS)
        for k in range(1, 16):
            emit_cx_mul(c, p16r[:, cdsl, k], p16i[:, cdsl, k],
                        p16r[:, cdsl, k - 1], p16i[:, cdsl, k - 1],
                        p16r[:, cdsl, 0], p16i[:, cdsl, 0], ttg1[:], ttg2[:])
        tick(c.gp.tensor_sub(powAiN[:, cdsl, :], zz2[:], powAi[:, cdsl, :]), c.gS)
        tick(c.gp.tensor_sub(p16iN[:, cdsl, :], zz2[:], p16i[:, cdsl, :]), c.gS)


    onesb_ap = sb_onesb[:]

    tick(c.ve.memset(ones512[:], 1.0), c.vS)
    tick(c.ve.memset(carry[:], 0.0), c.vS)
    bndflat = bnd[:].rearrange("p a b -> p (a b)")
    tick(c.gp.iota(bndflat, pattern=[[1, T]], base=1, channel_multiplier=0,
                   allow_small_or_imprecise_dtypes=True), dGi, 1)
    _emit_expert_casts()
    c.sc.wait_ge(dGi.h, 1)
    tick(c.sc.activation(bndflat, bndflat, AF.Sqrt), c.sS)
    c.ve.wait_ge(c.sS.h, c.sS.n)
    tick(c.ve.reciprocal(bndflat, bndflat), c.vS)
    tick(c.ve.tensor_copy(posrow[:], bndflat), c.vS)

    QM = [
        (0, [(0, 0, +1), (1, 1, -1), (2, 2, -1), (3, 3, -1)]),
        (1, [(0, 1, +1), (1, 0, +1), (2, 3, +1), (3, 2, -1)]),
        (2, [(0, 2, +1), (1, 3, -1), (2, 0, +1), (3, 1, +1)]),
        (3, [(0, 3, +1), (1, 2, +1), (2, 1, -1), (3, 0, +1)]),
    ]
    QMC = [
        (0, [(0, 0, +1), (1, 1, +1), (2, 2, +1), (3, 3, +1)]),
        (1, [(1, 0, +1), (3, 2, +1), (0, 1, -1), (2, 3, -1)]),
        (2, [(2, 0, +1), (1, 3, +1), (0, 2, -1), (3, 1, -1)]),
        (3, [(3, 0, +1), (2, 1, +1), (0, 3, -1), (1, 2, -1)]),
    ]

    def emit_quat(table, a_t, b_t, o_t, qh):
        def pl(comp):
            return comp * 2 + qh
        v = c.ve
        c.gp.wait_ge(c.vS.h, c.vS.n)
        for oc, terms in table:
            (a0, b0, s0), (a1, b1, s1), (a2, b2, s2), (a3, b3, s3) = terms
            assert s0 > 0
            e_, sem, u1, u2 = v, c.vS, tq1[:], tq2[:]
            tick(e_.tensor_mul(u1, a_t[:, pl(a0), :], b_t[:, pl(b0), :]), sem)
            tick(e_.tensor_mul(u2, a_t[:, pl(a1), :], b_t[:, pl(b1), :]), sem)
            tick(e_.tensor_tensor(u1, u1, u2,
                                  ALU.add if s1 > 0 else ALU.subtract), sem)
            tick(e_.tensor_mul(u2, a_t[:, pl(a2), :], b_t[:, pl(b2), :]), sem)
            tick(e_.tensor_tensor(u1, u1, u2,
                                  ALU.add if s2 > 0 else ALU.subtract), sem)
            tick(e_.tensor_mul(u2, a_t[:, pl(a3), :], b_t[:, pl(b3), :]), sem)
            tick(e_.tensor_tensor(o_t[:, pl(oc), :], u1, u2,
                                  ALU.add if s3 > 0 else ALU.subtract), sem)

    bndbf = bnd[:].rearrange("p a b -> p (a b)").bitcast(BF16)

    def nbv(j):
        return bndbf[:, j * 2 * GL: j * 2 * GL + GL]

    kguard = [0, 0]
    kguard_g = [0, 0]
    vrt_guard = 0
    vrt_guard_g = [0]
    kv_ready = {}
    vv_ready = {}

    def emit_A_keys(g):
        p = g % 2
        gsl = slice(g * GL, (g + 1) * GL)
        for pl_ in range(8):
            bank = pl_ % 2
            c.bank_wait(bank)
            c.te.wait_ge(dWkv.h, 32)
            c.te.wait_ge(dWx.h, 128)
            for kc in range(8):
                mm = c.te.matmul(pb[bank][:, 0:GL],
                                 WkT[:, kc, pl_ * 128:(pl_ + 1) * 128],
                                 resid[:, kc, gsl], start=(kc == 0), stop=(kc == 7))
            tick(mm, c.tS)
            c.sc.wait_ge(c.tS.h, c.tS.n)
            c.sc.wait_ge(c.vS.h, kguard[p])
            c.sc.wait_ge(c.gS.h, kguard_g[p])
            tick(c.sc.activation(kys2[:, p, pl_, :], pb[bank][:, 0:GL], AF.Identity,
                                 bias=sp["bk"][:, pl_:pl_ + 1]), c.sS)
            c.bankg[bank] = (c.sS, c.sS.n)
        kv_ready[g] = c.sS.n

    def emit_A_values(g):
        gsl = slice(g * GL, (g + 1) * GL)
        for pl_ in range(8):
            bank = 2 + (pl_ % 2)
            c.bank_wait(bank)
            for kc in range(8):
                mm = c.te.matmul(pb[bank][:, 0:GL],
                                 WvT[:, kc, pl_ * 128:(pl_ + 1) * 128],
                                 resid[:, kc, gsl], start=(kc == 0), stop=(kc == 7))
            tick(mm, c.tS)
            c.sc.wait_ge(c.tS.h, c.tS.n)
            c.sc.wait_ge(c.vS.h, vrt_guard)
            c.sc.wait_ge(c.gS.h, vrt_guard_g[0])
            tick(c.sc.activation(vrt[:, pl_, :], pb[bank][:, 0:GL], AF.Identity,
                                 bias=sp["bv"][:, pl_:pl_ + 1]), c.sS)
            c.bankg[bank] = (c.sS, c.sS.n)
        vv_ready[g] = c.sS.n

    def emit_knorm(g):
        kk = kys2[:, g % 2]
        c.ve.wait_ge(c.sS.h, kv_ready[g])
        v = c.ve
        for qh in range(2):
            pls = [cc * 2 + qh for cc in range(4)]
            tick(v.tensor_mul(kn1[:], kk[:, pls[0], :], kk[:, pls[0], :]), c.vS)
            for cc in range(1, 4):
                tick(v.tensor_mul(kn2[:], kk[:, pls[cc], :], kk[:, pls[cc], :]), c.vS)
                tick(v.tensor_add(kn1[:], kn1[:], kn2[:]), c.vS)
            c.sc.wait_ge(c.vS.h, c.vS.n)
            tick(c.sc.activation(kn1[:], kn1[:], AF.Sqrt), c.sS)
            c.ve.wait_ge(c.sS.h, c.sS.n)
            tick(v.tensor_scalar_max(kn1[:], kn1[:], 1e-12), c.vS)
            tick(v.reciprocal(kn2[:], kn1[:]), c.vS)
            for cc in range(4):
                tick(v.tensor_mul(kk[:, pls[cc], :], kk[:, pls[cc], :], kn2[:]), c.vS)

    for g in range(NG):
        p = g % 2
        gsl = slice(g * GL, (g + 1) * GL)
        kys = kys2[:, p]
        if g == 0:
            emit_A_keys(0)
            emit_A_values(0)
            emit_knorm(0)
        if g + 1 < NG:
            emit_A_keys(g + 1)

        c.ve.wait_ge(c.sS.h, max(kv_ready[g], vv_ready[g]))
        for qh in range(2):
            emit_quat(QM, vrt, kys, bnd, qh)

        c.ve.wait_ge(c.vS.h, c.vS.n)
        c.ve.wait_ge(c.gS.h, c.gS.n)
        for pl_ in range(8):
            tick(c.ve.tensor_tensor_scan(bnd[:, pl_, :], ones512[:], bnd[:, pl_, :],
                                         carry[:, pl_:pl_ + 1], ALU.mult, ALU.add),
                 c.vS)
        c.ve.wait_ge(c.vS.h, c.vS.n)
        for pl_ in range(8):
            tick(c.ve.tensor_copy(carry[:, pl_:pl_ + 1], bnd[:, pl_, GL - 1:GL]), c.vS)
        for pl_ in range(8):
            tick(c.ve.tensor_mul(bnd[:, pl_, :], bnd[:, pl_, :], posrow[:, gsl]), c.vS)
        for qh in range(2):
            emit_quat(QMC, bnd, kys, vrt, qh)
        kguard[p] = c.vS.n
        kguard_g[p] = c.gS.n

        c.sc.wait_ge(c.gS.h, c.gS.n)
        c.ve.wait_ge(c.gS.h, c.gS.n)
        src_ = [vrt[:, j, :] for j in range(8)]
        nb = [nbv(j) for j in range(8)]
        gcs = [sp["lng"][:, j:j + 1] for j in range(8)]
        bcs = [sp["lnb"][:, j:j + 1] for j in range(8)]
        fill = (lambda g_=g + 1: emit_knorm(g_)) if g + 1 < NG else None
        emit_ln(c, src_, nb, onesb_ap, (4, 5), mbc[:], sbc[:], GL,
                out_aps=nb, gcols=gcs, bcols=bcs, tq=tq1[:], filler=fill)
        vrt_guard = c.vS.n
        vrt_guard_g[0] = c.gS.n
        sS_normed = c.sS.n

        if g + 1 < NG:
            emit_A_values(g + 1)

        for mc in range(8):
            bank = 6 + (mc % 2)
            c.bank_wait(bank)
            c.te.wait_ge(c.sS.h, sS_normed)
            c.te.wait_ge(dWwo.h, 16)
            for kc in range(8):
                mm = c.te.matmul(pb[bank][:, 0:GL],
                                 WoT[:, kc, mc * 128:(mc + 1) * 128],
                                 nbv(kc), start=(kc == 0), stop=(kc == 7))
            tick(mm, c.tS)
            c.sc.wait_ge(c.tS.h, c.tS.n)
            c.sc.wait_ge(c.vS.h, kguard[p])
            tick(c.sc.activation(kys[:, mc, :], pb[bank][:, 0:GL], AF.Identity,
                                 bias=sp["bo"][:, mc:mc + 1]), c.sS)
            c.bankg[bank] = (c.sS, c.sS.n)
            c.ve.wait_ge(c.sS.h, c.sS.n)
            tick(c.ve.scalar_tensor_tensor(kys[:, mc, :], resid[:, mc, gsl], 2.0,
                                           kys[:, mc, :], ALU.mult, ALU.add), c.vS)
        src_ = [kys[:, j, :] for j in range(8)]
        outs = [resid[:, j, gsl] for j in range(8)]
        emit_ln(c, src_, nb, onesb_ap, (4, 5), mbc[:], sbc[:], GL, out_aps=outs)
        kguard[p] = c.vS.n

    if upto == "l0a":
        _emit_debug_out(c, resid, out_t, l0, nc)
        return _finish(c, nc)

    l0.close()
    l0f = ExitStack()
    E1 = l0f.enter_context
    winB = E1(nc.sbuf_tensor("winB", [128, 8, 64], BF16))
    woutB = E1(nc.sbuf_tensor("woutB", [128, D], BF16))
    wstage = E1(nc.sbuf_tensor("wstage", [128, D], F32))
    win32 = E1(nc.sbuf_tensor("win32", [128, 8, 64], F32))
    s2 = E1(nc.sbuf_tensor("s2", [128, 2048], F32))
    g2 = E1(nc.sbuf_tensor("g2", [128, 2048], F32))
    g2b = E1(nc.sbuf_tensor("g2b", [128, 2048], BF16))
    h2 = E1(nc.sbuf_tensor("h2", [128, 2, 2048], F32))
    sf = E1(nc.sbuf_tensor("sf", [128, 8, 512], F32))
    nbf2 = E1(nc.sbuf_tensor("nbf2", [128, 8, 512], BF16))
    mbc2 = E1(nc.sbuf_tensor("mbc2", [128, 512], F32))
    sbc2 = E1(nc.sbuf_tensor("sbc2", [128, 512], F32))
    c.fence()

    tick(c.sy.dma_start(out=win32[:], in_=win.rearrange("(kc p) m -> p kc m", p=128)),
         dWf, 16)
    tick(c.sy.dma_start(out=wstage[:], in_=woutd), dWf, 16)
    c.ve.wait_ge(dWf.h, 32)
    tick(c.ve.tensor_copy(winB[:], win32[:]), c.vS)
    tick(c.ve.tensor_copy(woutB[:], wstage[:]), c.vS)
    vS_l0fw = c.vS.n

    for tt in range(8):
        bank = tt % 2
        half = tt // 4
        prow = slice(64 * half, 64 * half + 64)
        col = slice((tt % 4) * 512, (tt % 4) * 512 + 512)
        c.bank_wait(bank)
        c.te.wait_ge(c.vS.h, vS_l0fw)
        for kc in range(8):
            mm = c.te.matmul(pb[bank][prow, 0:512], winB[:, kc, 0:64],
                             resid[:, kc, tt * 512:(tt + 1) * 512],
                             start=(kc == 0), stop=(kc == 7))
        tick(mm, c.tS)
        c.sc.wait_ge(c.tS.h, c.tS.n)
        tick(c.sc.activation(s2[prow, col], pb[bank][prow, 0:512], AF.Identity,
                             bias=sp["binc"][prow, 0:1]), c.sS)
        c.bankg[bank] = (c.sS, c.sS.n)
    sS_s2 = c.sS.n

    c.sc.wait_ge(c.sS.h, sS_s2)
    tick(c.sc.activation(g2[:], s2[:], AF.Identity, scale=np.float32(0.0), bias=sp["b2d"][:, 0:1]),
         c.sS)
    for i in range(16):
        c.sc.wait_ge(c.vS.h, c.vS.n)
        tick(c.sc.activation(h2[:, i % 2, :], s2[:], AF.Relu,
                             scale=sp["w1d"][:, i:i + 1], bias=sp["b1d"][:, i:i + 1]),
             c.sS)
        c.ve.wait_ge(c.sS.h, c.sS.n)
        tick(c.ve.scalar_tensor_tensor(g2[:], h2[:, i % 2, :], sp["w2d"][:, i:i + 1],
                                       g2[:], ALU.mult, ALU.add), c.vS)
    tick(c.ve.tensor_copy(g2b[:], g2[:]), c.vS)
    vS_g2b = c.vS.n

    for tt in range(8):
        half = tt // 4
        prow = slice(64 * half, 64 * half + 64)
        col = slice((tt % 4) * 512, (tt % 4) * 512 + 512)
        for mc in range(8):
            bank = 2 + (mc % 2)
            c.bank_wait(bank)
            c.te.wait_ge(c.vS.h, vS_g2b)
            mm = c.te.matmul(pb[bank][:, 0:512], woutB[prow, mc * 128:(mc + 1) * 128],
                             g2b[prow, col], start=True, stop=True)
            tick(mm, c.tS)
            c.sc.wait_ge(c.tS.h, c.tS.n)
            c.sc.wait_ge(c.vS.h, c.vS.n)
            tick(c.sc.activation(sf[:, mc, :], pb[bank][:, 0:512], AF.Identity,
                                 bias=sp["boutc"][:, mc:mc + 1]), c.sS)
            c.bankg[bank] = (c.sS, c.sS.n)
            c.ve.wait_ge(c.sS.h, c.sS.n)
            tick(c.ve.scalar_tensor_tensor(sf[:, mc, :],
                                           resid[:, mc, tt * 512:(tt + 1) * 512],
                                           1.0, sf[:, mc, :], ALU.mult, ALU.add),
                 c.vS)
        src = [sf[:, j, :] for j in range(8)]
        nb = [nbf2[:, j, :] for j in range(8)]
        outs = [resid[:, j, tt * 512:(tt + 1) * 512] for j in range(8)]
        emit_ln(c, src, nb, onesb_ap, (4, 5), mbc2[:], sbc2[:], 512, out_aps=outs)

    if upto == "l0f":
        _emit_debug_out(c, resid, out_t, l0f, nc)
        return _finish(c, nc)

    l0f.close()
    l1a = ExitStack()
    E2 = l1a.enter_context
    xsb = {}
    for d_ in ("f", "b"):
        xsb["r" + d_] = E2(nc.sbuf_tensor("xsr" + d_, [128, 2, T], BF16))
        xsb["i" + d_] = E2(nc.sbuf_tensor("xsi" + d_, [128, 2, T], BF16))
    bur = E2(nc.sbuf_tensor("bur", [128, 2, T], F32))
    BT = {"r": E2(nc.sbuf_tensor("BTr", [128, 8, P], BF16)),
          "i": E2(nc.sbuf_tensor("BTi", [128, 8, P], BF16))}
    l1a_mid = ExitStack()
    bui = l1a_mid.enter_context(nc.sbuf_tensor("bui", [128, 2, T], F32))
    braw = bur[:, :, 0:D]
    biraw = bui[:, :, 0:D]
    btmp1 = bur[:, :, D:2 * D]
    btmp2 = bui[:, :, D:2 * D]
    c.fence()

    l1dump = {}
    if upto == "l1adbg":
        dOd3 = Sem(nc, "dOd3")
        l1dump["tabs"] = nc.dram_tensor("d_tabs", [128, 4, 64], F32,
                                        kind="ExternalOutput").ap()
        l1dump["sbs"] = nc.dram_tensor("d_sbs", [128, 4, 3], F32,
                                       kind="ExternalOutput").ap()
        l1dump["bt"] = nc.dram_tensor("d_bt", [128, 8, 512], F32,
                                      kind="ExternalOutput").ap()
        l1dump["bu"] = nc.dram_tensor("d_bu", [128, 4, T], F32,
                                      kind="ExternalOutput").ap()
        l1dump["xs"] = nc.dram_tensor("d_xs", [128, 4, T], F32,
                                      kind="ExternalOutput").ap()

        def l1d(dst_ap, src_ap):
            c.gp.wait_ge(c.vS.h, c.vS.n)
            c.gp.wait_ge(c.sS.h, c.sS.n)
            tick(c.gp.dma_start(out=dst_ap, in_=src_ap), dOd3, 16)
            c.ve.wait_ge(dOd3.h, dOd3.n)
            c.sc.wait_ge(dOd3.h, dOd3.n)
        l1d(l1dump["tabs"][:, :, 0:16], powAr[:])
        l1d(l1dump["tabs"][:, :, 16:32], powAi[:])
        l1d(l1dump["tabs"][:, :, 32:48], p16r[:])
        l1d(l1dump["tabs"][:, :, 48:64], p16i[:])
        l1d(l1dump["sbs"], sBs[:])

    for dnum, d_ in enumerate(("f", "b")):
        c.sy.wait_ge(c.vS.h, c.vS.n)
        tick(c.sy.dma_start(out=braw,
                            in_=brr[d_].rearrange("(a p) h -> p a h", p=128)),
             dWb, 16)
        tick(c.sy.dma_start(out=biraw,
                            in_=bii[d_].rearrange("(a p) h -> p a h", p=128)),
             dWb, 16)
        c.ve.wait_ge(dWb.h, dWb.n)
        v = c.ve
        for ch in range(2):
            cd = dnum * 2 + ch
            sr = sBs[:, cd, 0:1]
            si = sBs[:, cd, 1:2]
            siN = sBs[:, cd, 2:3]
            sv(c, v.tensor_scalar_mul(btmp1[:, ch, :], braw[:, ch, :], sr))
            sv(c, v.scalar_tensor_tensor(btmp1[:, ch, :], biraw[:, ch, :], siN,
                                         btmp1[:, ch, :], ALU.mult, ALU.add))
            sv(c, v.tensor_scalar_mul(btmp2[:, ch, :], biraw[:, ch, :], sr))
            sv(c, v.scalar_tensor_tensor(btmp2[:, ch, :], braw[:, ch, :], si,
                                         btmp2[:, ch, :], ALU.mult, ALU.add))
        vS_bb = c.vS.n
        for comp, src_t in (("r", btmp1), ("i", btmp2)):
            for ch in range(2):
                for hb in range(8):
                    bank = hb % 2
                    c.bank_wait(bank)
                    c.te.wait_ge(c.vS.h, vS_bb)
                    tick(c.te.transpose(pb[bank][:, 0:128],
                                        src_t[:, ch, hb * 128:(hb + 1) * 128],
                                        sb_identf[:]), c.tS)
                    c.ve.wait_ge(c.tS.h, c.tS.n)
                    tick(c.ve.tensor_copy(BT[comp][:, hb, ch * 128:(ch + 1) * 128],
                                          pb[bank][:, 0:128]), c.vS)
                    c.bankg[bank] = (c.vS, c.vS.n)
        vS_bt = c.vS.n

        if upto == "l1adbg" and dnum == 0:
            c.gp.wait_ge(c.vS.h, c.vS.n)
            tick(c.gp.dma_start(out=l1dump["bt"][:, :, 0:256].bitcast(BF16)[:, :, 0:256]
                                if False else l1dump["bt"][:, :, 0:256],
                                in_=BT["r"][:]), dOd3, 16) if False else None
        vS_before_bu = c.vS.n
        for comp, dst in (("r", bur), ("i", bui)):
            for pc in range(2):
                for tt in range(8):
                    bank = 2 + (tt % 2)
                    c.bank_wait(bank)
                    c.te.wait_ge(c.vS.h, vS_bt)
                    for kc in range(8):
                        mm = c.te.matmul(pb[bank][:, 0:512],
                                         BT[comp][:, kc, pc * 128:(pc + 1) * 128],
                                         resid[:, kc, tt * 512:(tt + 1) * 512],
                                         start=(kc == 0), stop=(kc == 7))
                    tick(mm, c.tS)
                    c.sc.wait_ge(c.tS.h, c.tS.n)
                    c.sc.wait_ge(c.vS.h, vS_before_bu)
                    tick(c.sc.activation(dst[:, pc, tt * 512:(tt + 1) * 512],
                                         pb[bank][:, 0:512], AF.Copy), c.sS)
                    c.bankg[bank] = (c.sS, c.sS.n)
        c.ve.wait_ge(c.sS.h, c.sS.n)
        if upto == "l1adbg" and dnum == 0:
            l1d(l1dump["bu"][:, 0:2, :], bur[:])
            l1d(l1dump["bu"][:, 2:4, :], bui[:])
        c.ve.wait_ge(c.gS.h, c.gS.n)
        for ch in range(2):
            emit_cx_scan(c, bur[:, ch, :], bui[:, ch, :], tabs, dnum * 2 + ch,
                         reverse=(d_ == "b"))
        if upto == "l1adbg" and dnum == 0:
            l1d(l1dump["xs"][:, 0:2, :], bur[:])
            l1d(l1dump["xs"][:, 2:4, :], bui[:])
        for ch in range(2):
            tick(c.ve.tensor_copy(xsb["r" + d_][:, ch, :], bur[:, ch, :]), c.vS)
            tick(c.ve.tensor_copy(xsb["i" + d_][:, ch, :], bui[:, ch, :]), c.vS)

    l1a_mid.close()
    E2b = l1a.enter_context
    CT = {}
    for d_ in ("f", "b"):
        CT["r" + d_] = E2b(nc.sbuf_tensor("CTr" + d_, [128, 2, D], BF16))
        CT["i" + d_] = E2b(nc.sbuf_tensor("CTi" + d_, [128, 2, D], BF16))
    nb3b = E2b(nc.sbuf_tensor("nb3b", [128, 8, 512], BF16))
    mbc3 = E2b(nc.sbuf_tensor("mbc3", [128, 512], F32))
    sbc3 = E2b(nc.sbuf_tensor("sbc3", [128, 512], F32))
    vS_pre_ct = c.vS.n
    c.gp.wait_ge(c.vS.h, vS_pre_ct)
    for d_ in ("f", "b"):
        for comp, csrc in (("r", crr[d_]), ("i", cii[d_])):
            tick(c.gp.dma_start(out=CT[comp + d_][:],
                                in_=csrc.rearrange("(pc p) m -> p pc m", p=128)),
                 dWct, 16)
    c.ve.wait_ge(dWct.h, 64)
    for d_ in ("f", "b"):
        cta = CT["i" + d_][:].rearrange("p a b -> p (a b)")
        tick(c.ve.tensor_scalar_mul(cta, cta, -1.0), c.vS)
    diagD = {"f": E2b(nc.sbuf_tensor("diagDf", [128, 8, 128], BF16)),
             "b": E2b(nc.sbuf_tensor("diagDb", [128, 8, 128], BF16))}
    for d_ in ("f", "b"):
        dk = "dcf" if d_ == "f" else "dcb"
        for mc in range(8):
            tick(c.ve.tensor_scalar_mul(diagD[d_][:, mc, :], sb_identb[:],
                                        sp[dk][:, mc:mc + 1]), c.vS)
    sfv = bur[:].rearrange("p a t -> p (a t)").rearrange("p (m w) -> p m w", w=512)
    xs_done = c.vS.n
    sf_guard = xs_done
    for tt in range(8):
        tsl = slice(tt * 512, (tt + 1) * 512)
        for mc in range(8):
            for dnum, d_ in enumerate(("f", "b")):
                bank = 4 + ((mc + dnum) % 2)
                c.bank_wait(bank)
                c.te.wait_ge(c.vS.h, xs_done)
                first = True
                for comp in ("r", "i"):
                    for pc in range(2):
                        mm = c.te.matmul(pb[bank][:, 0:512],
                                         CT[comp + d_][:, pc, mc * 128:(mc + 1) * 128],
                                         xsb[comp + d_][:, pc, tsl],
                                         start=first, stop=False)
                        first = False
                mm = c.te.matmul(pb[bank][:, 0:512], diagD[d_][:, mc, :],
                                 resid[:, mc, tsl], start=False, stop=True)
                tick(mm, c.tS)
                c.sc.wait_ge(c.tS.h, c.tS.n)
                if dnum == 0:
                    c.sc.wait_ge(c.vS.h, sf_guard)
                    tick(c.sc.activation(sfv[:, mc, :], pb[bank][:, 0:512], AF.Gelu),
                         c.sS)
                    c.bankg[bank] = (c.sS, c.sS.n)
                else:
                    c.sc.wait_ge(c.vS.h, c.vS.n)
                    tick(c.sc.activation(sbc3[:], pb[bank][:, 0:512], AF.Gelu), c.sS)
                    c.bankg[bank] = (c.sS, c.sS.n)
                    c.ve.wait_ge(c.sS.h, c.sS.n)
                    tick(c.ve.tensor_add(sfv[:, mc, :], sfv[:, mc, :], sbc3[:]), c.vS)
                    tick(c.ve.scalar_tensor_tensor(sfv[:, mc, :], resid[:, mc, tsl],
                                                   1.0, sfv[:, mc, :], ALU.mult,
                                                   ALU.add), c.vS)
        src = [sfv[:, j, :] for j in range(8)]
        nb = [nb3b[:, j, :] for j in range(8)]
        outs = [resid[:, j, tsl] for j in range(8)]
        emit_ln(c, src, nb, onesb_ap, (6, 7), mbc3[:], sbc3[:], 512, out_aps=outs)
        sf_guard = c.vS.n

    if upto in ("l1a", "l1adbg"):
        _emit_debug_out(c, resid, out_t, l1a, nc)
        return _finish(c, nc)

    l1a.close()
    l1f = ExitStack()
    E3 = l1f.enter_context
    wsl = E3(nc.sbuf_tensor("wsl", [128, 4, 8, D], BF16))
    outq = E3(nc.sbuf_tensor("outq", [128, 8, 1024], F32))
    hbuf = E3(nc.sbuf_tensor("hbuf", [128, 2, 8, 512], BF16))
    hbuf_guard = [0, 0]
    wbc = E3(nc.sbuf_tensor("wbc", [128, 8, 1024], BF16))
    rt = E3(nc.sbuf_tensor("rt", [128, 8, 8], F32))
    rr1 = E3(nc.sbuf_tensor("rr1", [128, 8, 8], F32))
    rr2 = E3(nc.sbuf_tensor("rr2", [128, 8, 8], F32))
    rr3 = E3(nc.sbuf_tensor("rr3", [128, 8, 1], F32))
    rr3m = E3(nc.sbuf_tensor("rr3m", [128, 8, 1], U8))
    rr4 = E3(nc.sbuf_tensor("rr4", [128, 8, 1], F32))
    wT8 = E3(nc.sbuf_tensor("wT8", [8, 1024], BF16))
    mbc4 = E3(nc.sbuf_tensor("mbc4", [128, 512], F32))
    st1 = E3(nc.sbuf_tensor("st1", [128, 2, 512], F32))
    st1_guard = [0, 0]
    nb4 = None
    sbc4 = st1[:, 0, :] if False else None
    itmp = E3(nc.sbuf_tensor("itmp", [128, 1], F32))
    sb_bcast8 = E3(nc.sbuf_tensor("sb_bcast8", [8, D], BF16))
    tick(c.sy.dma_start(out=sb_bcast8[:], in_=bcast8), dWs, 16)
    c.te.wait_ge(dWs.h, dWs.n)
    ytm = hbuf[:, 1, :, :].rearrange("p a b -> p (a b)").bitcast(F32) \
        .rearrange("p (a b) -> p a b", a=2)
    c.fence()

    c.ve.wait_ge(dWs.h, dWs.n)
    tick(c.ve.reciprocal(itmp[:], sp["tempc"][:]), c.vS)

    slot_guard = [0, 0, 0, 0]
    slot_ready = [None] * 4
    state = {"nslot": 0}
    pending = [(e, m) for _q in range(4) for e in range(8) for m in range(3)]
    slq = []

    def load_one():
        em = pending.pop(0)
        sl = state["nslot"] % 4
        state["nslot"] += 1
        if slot_guard[sl]:
            c.sy.wait_ge(c.tS.h, slot_guard[sl])
        c.sy.wait_ge(dGe.h, 16 * 24)
        tick(c.sy.dma_start(out=wsl[:, sl, :, :],
                            in_=wbf_h[em[0], em[1]].rearrange("(kc p) i -> p kc i",
                                                              p=128)),
             dWsl[sl], 16)
        slot_ready[sl] = (dWsl[sl], dWsl[sl].n)
        slq.append((em, sl))

    def take_load(e, m):
        em, sl = slq.pop(0)
        assert em == (e, m), (em, e, m)
        return sl

    def refill(n=3):
        for _ in range(n):
            if pending:
                load_one()

    refill(3)

    rstate = {}

    def routing_part1(q):
        c.bank_wait(6)
        c.te.wait_ge(dGr.h, 16)
        for tm in range(8):
            for kc in range(8):
                mm = c.te.matmul(pb[6][:, tm * 8:(tm + 1) * 8],
                                 resid[:, kc,
                                       q * 1024 + tm * 128:q * 1024 + (tm + 1) * 128],
                                 wrB[:, kc, :], start=(kc == 0), stop=(kc == 7))
        tick(mm, c.tS)
        c.ve.wait_ge(c.tS.h, c.tS.n)
        tick(c.ve.tensor_copy(rt[:].rearrange("p a b -> p (a b)"), pb[6][:, 0:64]),
             c.vS)
        c.bankg[6] = (c.vS, c.vS.n)
        v = c.ve
        rta, r1a, r2a = rt[:], rr1[:], rr2[:]
        c.sc.wait_ge(c.vS.h, c.vS.n)
        tick(c.sc.activation(rta, rta, AF.Sigmoid, scale=itmp[:, 0:1]), c.sS)
        c.ve.wait_ge(c.sS.h, c.sS.n)
        sv(c, v.tensor_scalar(r1a, rta, GOLD_L, None, ALU.is_ge))
        sv(c, v.tensor_scalar(r2a, rta, GOLD_U, None, ALU.is_le))
        sv(c, v.tensor_mul(r1a, r1a, r2a))
        sv(c, v.tensor_scalar_add(rta, rta, -GOLD_C))
        c.sc.wait_ge(c.vS.h, c.vS.n)
        tick(c.sc.activation(rta, rta, AF.Abs), c.sS)
        c.ve.wait_ge(c.sS.h, c.sS.n)
        c.sc.wait_ge(c.vS.h, c.vS.n)
        tick(c.sc.activation(r2a, rta, AF.Exp, scale=np.float32(-10.0)), c.sS)
        tick(c.sc.activation(rta, rta, AF.Exp, scale=np.float32(-1.0 / 0.3)), c.sS)
        c.ve.wait_ge(c.sS.h, c.sS.n)
        sv(c, v.tensor_mul(r1a, r1a, r2a))
        sv(c, v.reduce_sum(rr3[:], r1a, axis=AXX))
        sv(c, v.tensor_scalar(rr3[:], rr3[:], 1e-8, None, ALU.is_lt))
        sv(c, v.reduce_max(rr4[:], rta, axis=AXX))
        rr4b = rr4[:, :, 0:1].broadcast_to([128, 8, 8])
        rr3b = rr3[:, :, 0:1].broadcast_to([128, 8, 8])
        sv(c, v.tensor_tensor(r2a, rta, rr4b, ALU.is_equal))
        sv(c, v.scalar_tensor_tensor(r2a, r2a, -1e30, rta, ALU.mult, ALU.add))
        sv(c, v.reduce_max(rr4[:], r2a, axis=AXX))
        sv(c, v.tensor_tensor(r2a, rta, rr4b, ALU.is_ge))
        sv(c, v.tensor_mul(r2a, r2a, rta))
        sv(c, v.reduce_sum(rr4[:], r2a, axis=AXX))
        sv(c, v.tensor_scalar_max(rr4[:], rr4[:], 1e-8))
        sv(c, v.reciprocal(rr4[:], rr4[:]))
        sv(c, v.tensor_tensor(r2a, r2a, rr4b, ALU.mult))
        sv(c, v.tensor_sub(rta, r2a, r1a))
        sv(c, v.tensor_tensor(rta, rta, rr3b, ALU.mult))
        sv(c, v.tensor_add(r1a, r1a, rta))
        sv(c, v.reduce_sum(rr4[:], r1a, axis=AXX))
        sv(c, v.tensor_scalar_max(rr4[:], rr4[:], 1e-8))
        sv(c, v.reciprocal(rr4[:], rr4[:]))
        sv(c, v.tensor_tensor(r1a, r1a, rr4b, ALU.mult))
        rstate["vS_w"] = c.vS.n

    def routing_part2():
        v = c.ve
        vS_w = rstate["vS_w"]
        for tm in range(8):
            bank = 6 + (tm % 2)
            c.bank_wait(bank)
            c.te.wait_ge(c.vS.h, vS_w)
            tick(c.te.transpose(pb[bank][0:8, 0:128], rr1[:, tm, :], sb_identf[:]),
                 c.tS)
            c.ve.wait_ge(c.tS.h, c.tS.n)
            tick(v.tensor_copy(wT8[:, tm * 128:(tm + 1) * 128], pb[bank][0:8, 0:128]),
                 c.vS)
            c.bankg[bank] = (c.vS, c.vS.n)
        vS_wt8 = c.vS.n
        for e in range(8):
            for half in range(2):
                bank = 6 + half
                c.bank_wait(bank)
                c.te.wait_ge(c.vS.h, vS_wt8)
                tick(c.te.matmul(pb[bank][:, 0:512],
                                 sb_bcast8[:, e * 128:(e + 1) * 128],
                                 wT8[:, half * 512:(half + 1) * 512],
                                 start=True, stop=True), c.tS)
                c.ve.wait_ge(c.tS.h, c.tS.n)
                tick(v.tensor_copy(wbc[:, e, half * 512:(half + 1) * 512],
                                   pb[bank][:, 0:512]), c.vS)
                c.bankg[bank] = (c.vS, c.vS.n)

    routing_part1(0)
    routing_part2()
    for q in range(4):
        c.ve.wait_ge(dOy[0].h, dOy[0].n)
        c.ve.wait_ge(dOy[1].h, dOy[1].n)
        def emit_wd(job):
            e_, tt2_, hb_, sld_, vS_h_ = job
            toff_ = q * 1024 + tt2_ * 512
            tsl_ = slice(toff_, toff_ + 512)
            for dc in range(8):
                bank = 4 + (dc % 2)
                c.bank_wait(bank)
                c.te.wait_ge(c.vS.h, vS_h_)
                c.te.wait_ge(slot_ready[sld_][0].h, slot_ready[sld_][1])
                for ic in range(8):
                    mm = c.te.matmul(pb[bank][:, 0:512],
                                     wsl[:, sld_, ic, dc * 128:(dc + 1) * 128],
                                     hbuf[:, hb_, ic, :], start=(ic == 0),
                                     stop=(ic == 7))
                tick(mm, c.tS)
                slot_guard[sld_] = c.tS.n
                c.ve.wait_ge(c.tS.h, c.tS.n)
                od = outq[:, dc, tt2_ * 512:(tt2_ + 1) * 512]
                if e_ == 0:
                    tick(c.ve.scalar_tensor_tensor(od, resid[:, dc, tsl_], 1.0,
                                                   pb[bank][:, 0:512], ALU.mult,
                                                   ALU.add), c.vS)
                else:
                    tick(c.ve.tensor_add(od, od, pb[bank][:, 0:512]), c.vS)
                c.bankg[bank] = (c.vS, c.vS.n)
            hbuf_guard[hb_] = c.tS.n

        pending_wd = None
        for e in range(8):
            slg = take_load(e, 0)
            slu = take_load(e, 1)
            sld = take_load(e, 2)
            for tt2 in range(2):
                toff = q * 1024 + tt2 * 512
                tsl = slice(toff, toff + 512)
                hb = (e * 2 + tt2) % 2
                for ic in range(8):
                    bg = ic % 2
                    bu_ = 2 + (ic % 2)
                    c.bank_wait(bg)
                    c.te.wait_ge(slot_ready[slg][0].h, slot_ready[slg][1])
                    for kc in range(8):
                        mm = c.te.matmul(pb[bg][:, 0:512],
                                         wsl[:, slg, kc, ic * 128:(ic + 1) * 128],
                                         resid[:, kc, tsl], start=(kc == 0),
                                         stop=(kc == 7))
                    tick(mm, c.tS)
                    slot_guard[slg] = c.tS.n
                    t_g = c.tS.n
                    c.bank_wait(bu_)
                    c.te.wait_ge(slot_ready[slu][0].h, slot_ready[slu][1])
                    for kc in range(8):
                        mm = c.te.matmul(pb[bu_][:, 0:512],
                                         wsl[:, slu, kc, ic * 128:(ic + 1) * 128],
                                         resid[:, kc, tsl], start=(kc == 0),
                                         stop=(kc == 7))
                    tick(mm, c.tS)
                    slot_guard[slu] = c.tS.n
                    t_u = c.tS.n
                    sb = ic % 2
                    c.sc.wait_ge(c.tS.h, t_g)
                    c.sc.wait_ge(c.vS.h, st1_guard[sb])
                    tick(c.sc.activation(st1[:, sb, :], pb[bg][:, 0:512], AF.Silu),
                         c.sS)
                    c.bankg[bg] = (c.sS, c.sS.n)
                    c.ve.wait_ge(c.sS.h, c.sS.n)
                    c.ve.wait_ge(c.tS.h, max(t_u, hbuf_guard[hb]))
                    tick(c.ve.tensor_mul(st1[:, sb, :], st1[:, sb, :],
                                         pb[bu_][:, 0:512]), c.vS)
                    c.bankg[bu_] = (c.vS, c.vS.n)
                    tick(c.ve.tensor_mul(hbuf[:, hb, ic, :], st1[:, sb, :],
                                         wbc[:, e, tt2 * 512:(tt2 + 1) * 512]), c.vS)
                    st1_guard[sb] = c.vS.n
                vS_h = c.vS.n
                if pending_wd is not None:
                    emit_wd(pending_wd)
                pending_wd = (e, tt2, hb, sld, vS_h)
            refill(3)
        if q < 3:
            routing_part1(q + 1)
        emit_wd(pending_wd)
        if q < 3:
            routing_part2()
        pending_wd = None

        for tt2 in range(2):
            toff = q * 1024 + tt2 * 512
            src = [outq[:, j, tt2 * 512:(tt2 + 1) * 512] for j in range(8)]
            nb = [hbuf[:, 0, j, :] for j in range(8)]
            fin = emit_ln(c, src, nb, onesb_ap, (6, 7), mbc4[:], st1[:, 0, :], 512,
                          out_aps=src)
            st1_guard[0] = c.vS.n
            hbuf_guard[0] = c.tS.n
            for tm in range(4):
                ybuf = tm % 2
                for dc in range(8):
                    bank = tm % 2
                    c.bank_wait(bank)
                    c.te.wait_ge(c.vS.h, fin[dc])
                    tick(c.te.transpose(pb[bank][:, 0:128],
                                        outq[:, dc,
                                             tt2 * 512 + tm * 128:tt2 * 512 + (tm + 1) * 128],
                                        sb_identf[:]), c.tS)
                    c.sc.wait_ge(c.tS.h, c.tS.n)
                    c.sc.wait_ge(dOy[ybuf].h, dOy[ybuf].n)
                    tick(c.sc.activation(ytm[:, ybuf, dc * 128:(dc + 1) * 128],
                                         pb[bank][:, 0:128], AF.Copy), c.sS)
                    c.bankg[bank] = (c.sS, c.sS.n)
                c.sc.wait_ge(c.sS.h, c.sS.n)
                tick(c.sc.dma_start(out=out_t[toff + tm * 128:toff + (tm + 1) * 128, :],
                                    in_=ytm[:, ybuf, :]), dOy[ybuf], 16)

    c.sc.wait_ge(dOy[0].h, dOy[0].n)
    c.sc.wait_ge(dOy[1].h, dOy[1].n)
    return _finish(c, nc)


def _emit_debug_out(c, resid, out_t, scope, nc):
    dbg = scope.enter_context(nc.sbuf_tensor("dbg", [128, 1024], F32))
    flat = out_t.flatten()
    c.ve.wait_ge(c.dWx.h, 128)
    for ch in range(8):
        for qq in range(4):
            c.ve.wait_ge(c.sS.h, c.sS.n)
            c.ve.wait_ge(c.dO.h, c.dO.n)
            tick(c.ve.tensor_copy(dbg[:], resid[:, ch, qq * 1024:(qq + 1) * 1024]),
                 c.vS)
            c.sc.wait_ge(c.vS.h, c.vS.n)
            off = ch * 128 * T + qq * 1024
            tick(c.sc.dma_start(out=flat[off:off + 128 * T // 4]
                                .rearrange("(p t) -> p t", p=128, t=1024)
                                if False else
                                flat[ch * 128 * T:(ch + 1) * 128 * T]
                                .rearrange("(p t) -> p t", p=128)
                                [:, qq * 1024:(qq + 1) * 1024],
                                in_=dbg[:]), c.dO, 16)
    c.sc.wait_ge(c.dO.h, c.dO.n)


def _finish(c, nc):
    c.run()
    return nc


def _perm_planar():
    perm = np.zeros(D, dtype=np.int64)
    for cc in range(4):
        for qh in range(2):
            for j in range(128):
                perm[(cc * 2 + qh) * 128 + j] = 4 * (qh * 128 + j) + cc
    return perm


def _cols(vec, ncol):
    return np.ascontiguousarray(np.asarray(vec, dtype=np.float32)
                                .reshape(ncol, 128).T)


def prepare_in_maps(inp):
    f32 = lambda a: np.ascontiguousarray(np.asarray(a, dtype=np.float32))
    perm = _perm_planar()
    sh = {}
    sh["wk"] = np.ascontiguousarray(f32(inp["L0_attn_Wk"])[perm, :].T)
    sh["bk"] = _cols(f32(inp["L0_attn_bk"])[perm], 8)
    sh["wv"] = np.ascontiguousarray(f32(inp["L0_attn_Wv"])[perm, :].T)
    sh["bv"] = _cols(f32(inp["L0_attn_bv"])[perm], 8)
    sh["lng"] = _cols(f32(inp["L0_attn_ln_g"])[perm], 8)
    sh["lnb"] = _cols(f32(inp["L0_attn_ln_b"])[perm], 8)
    sh["wo"] = np.ascontiguousarray(f32(inp["L0_attn_Wo"])[:, perm].T)
    sh["bo"] = _cols(f32(inp["L0_attn_bo"]), 8)
    sh["win"] = f32(inp["L0_ffn_Win"])
    sh["binc"] = np.concatenate([f32(inp["L0_ffn_bin_"])] * 2).reshape(128, 1)
    sh["w1d"] = np.vstack([f32(inp["L0_ffn_W1"])] * 2)
    sh["b1d"] = np.vstack([f32(inp["L0_ffn_b1"])] * 2)
    sh["w2d"] = np.vstack([f32(inp["L0_ffn_W2"])] * 2)
    sh["b2d"] = np.concatenate([f32(inp["L0_ffn_b2"])] * 2).reshape(128, 1)
    sh["woutd"] = np.vstack([f32(inp["L0_ffn_Wout"])] * 2)
    sh["boutc"] = _cols(f32(inp["L0_ffn_bout"]), 8)
    for d_ in ("f", "b"):
        pf = "L1_attn_fwd_" if d_ == "f" else "L1_attn_bwd_"
        sh["lr" + d_] = _cols(f32(inp[pf + "log_real"]), 2)
        sh["im" + d_] = _cols(f32(inp[pf + "imag"]), 2)
        sh["ld" + d_] = _cols(f32(inp[pf + "log_Delta"]), 2)
        sh["br" + d_] = f32(inp[pf + "B_real"])
        sh["bi" + d_] = f32(inp[pf + "B_imag"])
        sh["cr" + d_] = np.ascontiguousarray(f32(inp[pf + "C_real"]).T)
        sh["ci" + d_] = np.ascontiguousarray(f32(inp[pf + "C_imag"]).T)
        sh["dc" + d_] = _cols(f32(inp[pf + "D"]), 8)
    sh["wg"] = f32(inp["L1_ffn_Wg"])
    sh["wu"] = f32(inp["L1_ffn_Wu"])
    sh["wd"] = f32(inp["L1_ffn_Wd"])
    sh["wr"] = f32(inp["L1_ffn_Wr"])
    sh["tempc"] = np.repeat(f32(inp["L1_ffn_temperature"]), 128).reshape(128, 1)
    sh["identf"] = np.eye(128, dtype=np.float32)
    sh["identb"] = np.eye(128, dtype=ml_dtypes.bfloat16)
    sh["onesb"] = np.ones((128, 128), dtype=ml_dtypes.bfloat16)
    b8 = np.zeros((8, D), dtype=ml_dtypes.bfloat16)
    for e in range(8):
        b8[e, e * 128:(e + 1) * 128] = 1
    sh["bcast8"] = b8

    x_full = f32(inp["L0_attn_x"])
    return [dict(sh, x=np.ascontiguousarray(x_full[core])) for core in range(NCORE)]


_NC_CACHE = {}


def kernel(**inp):
    in_maps = prepare_in_maps(inp)
    if "all" not in _NC_CACHE:
        _NC_CACHE["all"] = build_program("all")
    nc = _NC_CACHE["all"]
    res = run_bass_kernel_spmd(nc, in_maps, core_ids=list(range(NCORE)))
    return np.stack([res.results[i]["out"] for i in range(NCORE)], axis=0)
